# version 38
# baseline (speedup 1.0000x reference)
import numpy as np
import concourse.bass as bass
import concourse.mybir as mybir
from concourse.bass_utils import run_bass_kernel_spmd

F32 = mybir.dt.float32
BF16 = mybir.dt.bfloat16
AF = mybir.ActivationFunctionType
ALU = mybir.AluOpType

D = 1024
S = 2048
B = 8
CH = 8
HF = S // 2
NT = 4
EPS = 1e-5

ENGINES = ("tensor", "vector", "scalar", "sync", "gpsimd")

VEC_NAMES = ["af", "ab", "bxf", "bxb", "bgxf", "bgxb", "bgaf", "bgab",
             "bof", "bob", "ldb", "fsc", "fbi"]
VCOL = {n: i * 8 for i, n in enumerate(VEC_NAMES)}


class Buf:
    __slots__ = ("name", "writers", "readers")

    def __init__(self, name, deps=()):
        self.name = name
        self.writers = []
        self.readers = []
        for d in deps:
            self.writers = self.writers + d.writers + d.readers

    def wrote(self, tok):
        self.writers = [tok]
        self.readers = []

    def read(self, tok):
        if tok is not None:
            self.readers.append(tok)


class Region:

    def __init__(self, name):
        self.name = name
        self.cur = []

    def new(self, prefix, n=1):
        bs = [Buf(f"{prefix}{i}", self.cur) for i in range(n)]
        self.cur = bs
        return bs


class Builder:
    DMA_POOL = 8

    def __init__(self, nc):
        self.nc = nc
        self.q = {e: [] for e in ENGINES}
        self.sems = {}
        self.counts = {}
        self.waited = {e: {} for e in ENGINES}

    def sem(self, name):
        if name not in self.sems:
            self.sems[name] = self.nc.alloc_semaphore(name)
            self.counts[name] = 0
        return self.sems[name]

    def _need(self, reads, writes):
        need = {}
        for b in reads:
            for (s, v) in b.writers:
                if v > need.get(s, 0):
                    need[s] = v
        for b in writes:
            for (s, v) in b.writers + b.readers:
                if v > need.get(s, 0):
                    need[s] = v
        return need

    def _filter(self, eng, need):
        wl = []
        for s, v in sorted(need.items()):
            if eng == "tensor" and s == "pe":
                continue
            if self.waited[eng].get(s, 0) < v:
                wl.append((s, v))
                self.waited[eng][s] = v
        return wl

    def op(self, eng, emit, reads=(), writes=(), inc=None, step=1):
        if inc in ("dX", "dW", "dO"):
            k = f"_{inc}_rr"
            j = self.counts.get(k, 0)
            self.counts[k] = j + 1
            inc = f"{inc}{j % self.DMA_POOL}"
            prev = self.counts.get(inc, 0)
            if prev and self.waited[eng].get(inc, 0) < prev:
                pb = Buf(f"_rr_{inc}_{prev}")
                pb.writers = [(inc, prev)]
                reads = list(reads) + [pb]
        return self.group(eng, [emit], reads=reads, writes=writes, inc=inc,
                          step=step)

    def group(self, eng, emits, reads=(), writes=(), inc=None, step=1):
        wl = self._filter(eng, self._need(reads, writes))
        tok = None
        if inc is not None:
            self.sem(inc)
            self.counts[inc] += step
            tok = (inc, self.counts[inc])
        sems = self.sems

        def thunk(e, wl=tuple(wl), emits=tuple(emits), tok=tok, step=step):
            for s, v in wl:
                e.wait_ge(sems[s], v)
            inst = None
            for em in emits:
                inst = em(e)
            if tok is not None:
                inst.then_inc(sems[tok[0]], step)

        self.q[eng].append(thunk)
        for b in reads:
            b.read(tok)
        for b in writes:
            b.wrote(tok)
        return tok

    def mm_group(self, tiles, tile_reads, group_reads=(), writes=(), inc="pe"):
        head = self._filter("tensor", self._need(group_reads, writes))
        per = []
        for trs in tile_reads:
            per.append(tuple(self._filter("tensor", self._need(trs, ()))))
        self.sem(inc)
        self.counts[inc] += 1
        tok = (inc, self.counts[inc])
        sems = self.sems

        def thunk(e, head=tuple(head), per=tuple(per), tiles=tuple(tiles),
                  tok=tok):
            for s, v in head:
                e.wait_ge(sems[s], v)
            inst = None
            for (lh, rh, oa, st, sp), wl in zip(tiles, per):
                for s, v in wl:
                    e.wait_ge(sems[s], v)
                inst = e.matmul(oa, lh, rh, start=st, stop=sp)
            inst.then_inc(sems[tok[0]], 1)

        self.q["tensor"].append(thunk)
        seen = set()
        for trs in tile_reads:
            for b in trs:
                if id(b) not in seen:
                    seen.add(id(b))
                    b.read(tok)
        for b in group_reads:
            b.read(tok)
        for b in writes:
            b.wrote(tok)
        return tok

    def raw(self, eng, emit):
        self.q[eng].append(lambda e, emit=emit: emit(e))


def build_program():
    nc = bass.Bass("TRN2", target_bir_lowering=False)

    x_hbm = nc.dram_tensor("x", [D, S], F32, kind="ExternalInput")
    w_hbm = {}
    for wn in ["Wxf", "Wxb", "Wgxf", "Wgxb", "Wgaf", "Wgab", "Wof", "Wob",
               "Wld", "Wffn"]:
        w_hbm[wn] = nc.dram_tensor(wn, [128, CH * D], F32, kind="ExternalInput")
    vecs_hbm = nc.dram_tensor("vecs", [128, 8 * len(VEC_NAMES)], F32,
                              kind="ExternalInput")
    out_hbm = nc.dram_tensor("out", [D, S], F32, kind="ExternalOutput")

    A1 = nc.alloc_sbuf_tensor("A1", [128, 16384], BF16)
    A2 = nc.alloc_sbuf_tensor("A2", [128, 16384], BF16)
    A3 = nc.alloc_sbuf_tensor("A3", [128, 16384], BF16)
    A4 = nc.alloc_sbuf_tensor("A4", [128, 16384], BF16)
    A5 = nc.alloc_sbuf_tensor("A5", [128, 26624], BF16)
    vecs_sb = nc.alloc_sbuf_tensor("vecs_sb", [128, 8 * len(VEC_NAMES)], F32)
    dvec_sb = nc.alloc_sbuf_tensor("dvec_sb", [128, 48], F32)
    ones_sb = nc.alloc_sbuf_tensor("ones_sb", [128, 1], BF16)
    onesf_sb = nc.alloc_sbuf_tensor("onesf_sb", [1, 128], F32)
    onesb_sb = nc.alloc_sbuf_tensor("onesb_sb", [1, 128], BF16)
    srbf = nc.alloc_sbuf_tensor("srbf", [1, 4096], BF16)
    srows = nc.alloc_sbuf_tensor("srows", [1, 4112], F32)

    psA = nc.alloc_psum_tensor("psA", [128, 2048], F32)
    psB = nc.alloc_psum_tensor("psB", [128, 2048], F32)

    def f32v(arena, byte_off, n):
        return arena.ap()[:, byte_off // 2: byte_off // 2 + 2 * n].bitcast(F32)

    R1 = 0
    R2 = 16384
    R3 = 24576
    R4 = 32768
    R5 = 36864
    R6 = 45056

    bd = Builder(nc)
    for base in ("dW", "dX", "dO"):
        for j in range(Builder.DMA_POOL):
            bd.sem(f"{base}{j}")
    for sname in ("pe", "dv", "sc", "gp"):
        bd.sem(sname)

    rA1 = Region("A1")
    rA2 = Region("A2")
    rA3 = Region("A3")
    rA4lo = Region("A4lo")
    rA4hi = Region("A4hi")
    rR1e = [Region(f"R1e{j}") for j in range(8)]
    rR2 = Region("R2")
    rR3h = [Region(f"R3h{j}") for j in range(2)]
    rR5h = [Region(f"R5h{j}") for j in range(2)]

    def new_over(regions, prefix):
        deps = [b for r in regions for b in r.cur]
        b = Buf(prefix, deps)
        for r in regions:
            r.cur = [b]
        return b

    vecsB = Buf("vecs")
    dvecB = Buf("dvec")
    onesB = Buf("ones")
    PS = [Buf("ps0"), Buf("ps1"), Buf("ps2"), Buf("ps3")]
    ps_t = {0: psA, 1: psA, 2: psB, 3: psB}
    ps_off = {0: 0, 1: 1024, 2: 0, 3: 1024}

    def ps_ap(slot, lo, hi):
        return ps_t[slot].ap()[:, ps_off[slot] + lo: ps_off[slot] + hi]

    def vcol(nm, o):
        return vecs_sb[:, VCOL[nm] + o: VCOL[nm] + o + 1]

    def ae_col(d, o):
        base = 0 if d == 0 else 8
        return dvec_sb[:, base + o: base + o + 1]

    bd.op("sync", lambda e: e.dma_start(vecs_sb[:], vecs_hbm.ap()),
          writes=[vecsB], inc="dX", step=16)
    eps_ap = srows.ap()[0:1, 4096:4097]
    epsB = Buf("eps")
    bd.op("vector", lambda e: e.memset(eps_ap, EPS), writes=[epsB], inc="dv")

    WxfS = rA4lo.new("Wxf", CH)
    WxbS = rA4hi.new("Wxb", CH)
    for oc in range(CH):
        bd.op("gpsimd",
              lambda e, oc=oc:
              e.dma_start(A4.ap()[:, oc * 1024:(oc + 1) * 1024],
                          w_hbm["Wxf"].ap()[:, oc * 1024:(oc + 1) * 1024]),
              writes=[WxfS[oc]], inc="dW", step=16)

    def _wxb():
        for oc in range(CH):
            bd.op("gpsimd",
                  lambda e, oc=oc:
                  e.dma_start(A4.ap()[:, 8192 + oc * 1024: 8192 + (oc + 1) * 1024],
                              w_hbm["Wxb"].ap()[:, oc * 1024:(oc + 1) * 1024]),
                  writes=[WxbS[oc]], inc="dW", step=16)
    wxb_load = [_wxb]

    xgH = [rA1.new(f"xg{c}", 2) for c in range(CH)]
    xcP1 = ([new_over(rR1e[2 * j:2 * j + 2], f"xcP1_{j}") for j in range(4)] +
            [rR3h[j].new(f"xcP1b{j}", 1)[0] for j in range(2)] +
            [rR5h[j].new(f"xcP1c{j}", 1)[0] for j in range(2)])
    xcP1_ap = ([f32v(A5, R1 + 4096 * i, 1024) for i in range(4)] +
               [f32v(A5, R3, 1024), f32v(A5, R3 + 4096, 1024),
                f32v(A5, R5, 1024), f32v(A5, R5 + 4096, 1024)])

    def p1_dma(i):
        c, hh = i // 2, i % 2
        sl = i % 8
        eng, q = (("sync", "dX") if i % 2 == 0 else ("scalar", "dO"))
        bd.op(eng,
              lambda e, c=c, hh=hh, sl=sl:
              e.dma_start(xcP1_ap[sl],
                          x_hbm.ap()[c * 128:(c + 1) * 128,
                                     hh * 1024:(hh + 1) * 1024]),
              writes=[xcP1[sl]], inc=q, step=16)

    for i in range(8):
        p1_dma(i)
    bd.op("scalar", lambda e: e.activation(dvec_sb[:, 0:8],
                                           vecs_sb[:, VCOL["af"]:VCOL["af"] + 8],
                                           AF.Sigmoid),
          reads=[vecsB], writes=[dvecB], inc="sc")
    bd.op("scalar", lambda e: e.activation(dvec_sb[:, 8:16],
                                           vecs_sb[:, VCOL["ab"]:VCOL["ab"] + 8],
                                           AF.Sigmoid),
          reads=[vecsB], writes=[dvecB], inc="sc")
    bd.op("vector", lambda e: e.tensor_add(dvec_sb[:, 16:24],
                                           vecs_sb[:, VCOL["bof"]:VCOL["bof"] + 8],
                                           vecs_sb[:, VCOL["bob"]:VCOL["bob"] + 8]),
          reads=[vecsB], writes=[dvecB], inc="dv")
    bd.op("vector", lambda e: e.tensor_scalar_mul(dvec_sb[:, 16:24],
                                                  dvec_sb[:, 16:24], 0.5),
          reads=[dvecB], writes=[dvecB], inc="dv")
    bd.op("vector", lambda e: e.tensor_scalar_mul(dvec_sb[:, 24:32],
                                                  vecs_sb[:, VCOL["fsc"]:VCOL["fsc"] + 8],
                                                  2.0),
          reads=[vecsB], writes=[dvecB], inc="dv")
    bd.op("vector", lambda e: e.tensor_scalar_add(dvec_sb[:, 32:40],
                                                  vecs_sb[:, VCOL["fbi"]:VCOL["fbi"] + 8],
                                                  100.0),
          reads=[vecsB], writes=[dvecB], inc="dv")
    bd.op("vector", lambda e: e.memset(dvec_sb[:, 40:41], 200.0),
          writes=[dvecB], inc="dv")
    bd.op("vector", lambda e: e.memset(ones_sb[:], 1.0), writes=[onesB], inc="dv")
    bd.op("vector", lambda e: e.memset(onesf_sb[:], 1.0), writes=[onesB], inc="dv")
    bd.op("vector", lambda e: e.memset(onesb_sb[:], 1.0), writes=[onesB], inc="dv")
    for i in range(2 * CH):
        c, hh = i // 2, i % 2
        sl = i % 8
        bd.op("scalar",
              lambda e, c=c, hh=hh, sl=sl:
              e.activation(A1.ap()[:, c * 2048 + hh * 1024: c * 2048 + (hh + 1) * 1024],
                           xcP1_ap[sl], AF.Gelu_apprx_tanh),
              reads=[xcP1[sl]], writes=[xgH[c][hh]], inc="sc")
        if i + 8 < 2 * CH:
            p1_dma(i + 8)
        if i == 7 and wxb_load:
            wxb_load.pop(0)()

    def wtile(arena_ap, o, k):
        return arena_ap[:, o * 1024 + k * 128: o * 1024 + (k + 1) * 128]

    def p2_direction(wview, wslabs, bias_name, xb_arena, xb_bufs):
        for o in range(CH):
            slots = (0, 1) if o % 2 == 0 else (2, 3)
            pst = psA if o % 2 == 0 else psB
            tiles, treads = [], []
            for k in range(CH):
                for t in range(NT):
                    tiles.append((wtile(wview, o, k),
                                  A1.ap()[:, k * 2048 + t * 512: k * 2048 + (t + 1) * 512],
                                  pst.ap()[:, t * 512:(t + 1) * 512],
                                  k == 0, k == CH - 1))
                    treads.append([xgH[k][t // 2], wslabs[o]])
            bd.mm_group(tiles, treads, group_reads=[],
                        writes=[PS[slots[0]], PS[slots[1]]])
            bd.op("scalar",
                  lambda e, o=o, pst=pst, bias_name=bias_name, xb_arena=xb_arena:
                  e.activation(xb_arena.ap()[:, o * 2048:(o + 1) * 2048], pst.ap()[:],
                               AF.Identity, bias=vcol(bias_name, o)),
                  reads=[PS[slots[0]], PS[slots[1]], vecsB],
                  writes=[xb_bufs[o]], inc="sc")

    xbf = rA2.new("xbf", CH)
    xbb = rA3.new("xbb", CH)
    p2_direction(A4.ap()[:, 0:8192], WxfS, "bxf", A2, xbf)
    p2_direction(A4.ap()[:, 8192:16384], WxbS, "bxb", A3, xbb)

    A4f = A4.ap().bitcast(F32)
    gx_ap = [A4f[:, 0:2048], f32v(A5, R1, 2048)]
    ga_ap = [A4f[:, 2048:4096], f32v(A5, R1 + 8192, 2048)]
    ab_ap = [A4f[:, 4096:6144], f32v(A5, R3, 2048)]
    tb_ap = [A4f[:, 6144:8192], f32v(A5, R5, 2048)]
    slab_ap = [A5.ap()[:, 8192 + i * 1024: 8192 + (i + 1) * 1024] for i in range(4)]

    scr0 = rA4lo.new("scr0", 2) + rA4hi.new("scr0h", 2)
    scr1 = [new_over(rR1e[0:4], "scr1gx"), new_over(rR1e[4:8], "scr1ga"),
            new_over(rR3h, "scr1ab"), new_over(rR5h, "scr1tb")]
    gxB = [scr0[0], scr1[0]]
    gaB = [scr0[1], scr1[1]]
    abB = [scr0[2], scr1[2]]
    tbB = [scr0[3], scr1[3]]
    slabs = rR2.new("slab", 4)

    def gates_direction(d, wgx_name, wga_name, bgx_name, bga_name, xb_arena,
                        xb_bufs, h_arena, h_bufs):
        def slab_dma(which, o):
            i = (0 if which == 0 else 2) + o % 2
            nm = wgx_name if which == 0 else wga_name
            bd.op("gpsimd",
                  lambda e, i=i, nm=nm, o=o:
                  e.dma_start(slab_ap[i], w_hbm[nm].ap()[:, o * 1024:(o + 1) * 1024]),
                  writes=[slabs[i]], inc="dW", step=16)

        slab_dma(0, 0)
        slab_dma(1, 0)
        for o in range(CH):
            if o + 1 < CH:
                slab_dma(0, o + 1)
                slab_dma(1, o + 1)
            p = o % 2
            igx = 0 + p
            iga = 2 + p
            tiles, treads = [], []
            for k in range(CH):
                for t in range(NT):
                    tiles.append((slab_ap[igx][:, k * 128:(k + 1) * 128],
                                  xb_arena.ap()[:, k * 2048 + t * 512: k * 2048 + (t + 1) * 512],
                                  psA.ap()[:, t * 512:(t + 1) * 512], k == 0, k == CH - 1))
                    treads.append([xb_bufs[k]])
            bd.mm_group(tiles, treads, group_reads=[slabs[igx]],
                        writes=[PS[0], PS[1]])
            tiles, treads = [], []
            for k in range(CH):
                for t in range(NT):
                    tiles.append((slab_ap[iga][:, k * 128:(k + 1) * 128],
                                  xb_arena.ap()[:, k * 2048 + t * 512: k * 2048 + (t + 1) * 512],
                                  psB.ap()[:, t * 512:(t + 1) * 512], k == 0, k == CH - 1))
                    treads.append([xb_bufs[k]])
            bd.mm_group(tiles, treads, group_reads=[slabs[iga]],
                        writes=[PS[2], PS[3]])
            bd.op("scalar",
                  lambda e, o=o, p=p, bgx_name=bgx_name:
                  e.activation(gx_ap[p], psA.ap()[:], AF.Sigmoid, bias=vcol(bgx_name, o)),
                  reads=[PS[0], PS[1], vecsB], writes=[gxB[p]], inc="sc")
            bd.op("scalar",
                  lambda e, o=o, p=p, bga_name=bga_name:
                  e.activation(ga_ap[p], psB.ap()[:], AF.Sigmoid, bias=vcol(bga_name, o)),
                  reads=[PS[2], PS[3], vecsB], writes=[gaB[p]], inc="sc")
            bd.op("vector",
                  lambda e, o=o, d=d, p=p:
                  e.tensor_single_scalar(ab_ap[p], ga_ap[p], ae_col(d, o), ALU.mult),
                  reads=[gaB[p], dvecB], writes=[abB[p]], inc="dv")
            bd.op("scalar", lambda e, p=p: e.activation(tb_ap[p], ab_ap[p], AF.Square),
                  reads=[abB[p]], writes=[tbB[p]], inc="sc")
            bd.op("scalar", lambda e, p=p: e.activation(tb_ap[p], tb_ap[p], AF.Sqrt,
                                                        bias=1.0, scale=-1.0),
                  reads=[tbB[p]], writes=[tbB[p]], inc="sc")
            bd.op("vector",
                  lambda e, o=o, p=p, xb_arena=xb_arena:
                  e.tensor_mul(gx_ap[p], gx_ap[p], xb_arena.ap()[:, o * 2048:(o + 1) * 2048]),
                  reads=[gxB[p], xb_bufs[o]], writes=[gxB[p]], inc="dv")
            bd.op("vector", lambda e, p=p: e.tensor_mul(gx_ap[p], gx_ap[p], tb_ap[p]),
                  reads=[gxB[p], tbB[p]], writes=[gxB[p]], inc="dv")
            hview = h_arena.ap()[:, o * 2048:(o + 1) * 2048]
            if d == 0:
                bd.op("vector",
                      lambda e, hview=hview, p=p:
                      e.tensor_tensor_scan(hview, ab_ap[p], gx_ap[p], 0.0,
                                           ALU.mult, ALU.add),
                      reads=[abB[p], gxB[p]], writes=[h_bufs[o]], inc="dv")
            else:
                bd.op("vector",
                      lambda e, hview=hview, p=p:
                      e.tensor_tensor_scan(hview[:, ::-1], ab_ap[p][:, ::-1],
                                           gx_ap[p][:, ::-1], 0.0, ALU.mult, ALU.add),
                      reads=[abB[p], gxB[p]], writes=[h_bufs[o]], inc="dv")

    rA1.cur = [b for pair in xgH for b in pair]
    hf = rA1.new("hf", CH)
    gates_direction(0, "Wgxf", "Wgaf", "bgxf", "bgaf", A2, xbf, A1, hf)
    hb = rA2.new("hb", CH)
    gates_direction(1, "Wgxb", "Wgab", "bgxb", "bgab", A3, xbb, A2, hb)

    WldS = rA4lo.new("Wld", CH)
    wld_q = list(range(CH))

    def wld_slab_load():
        if not wld_q:
            return
        oc = wld_q.pop(0)
        bd.op("gpsimd",
              lambda e, oc=oc:
              e.dma_start(A4.ap()[:, oc * 1024:(oc + 1) * 1024],
                          w_hbm["Wld"].ap()[:, oc * 1024:(oc + 1) * 1024]),
              writes=[WldS[oc]], inc="dW", step=16)
    sgnB = rA4hi.new("sgnW")[0]

    wo_slab_ap = slab_ap + [A5.ap()[:, R6 // 2 + i * 1024: R6 // 2 + (i + 1) * 1024]
                            for i in range(4)]
    woslabs = rR2.new("woslab", 4) + [Buf(f"woslabx{i}") for i in range(4)]
    wo_seq = [4]

    def wo_ring_load(nm, oc):
        i = wo_seq[0] % 8
        wo_seq[0] += 1
        bd.op("gpsimd",
              lambda e, i=i, nm=nm, oc=oc:
              e.dma_start(wo_slab_ap[i], w_hbm[nm].ap()[:, oc * 1024:(oc + 1) * 1024]),
              writes=[woslabs[i]], inc="dW", step=16)
        return i

    wt_ap = [f32v(A5, R6, 1024), f32v(A5, R6 + 4096, 1024)]
    wtB = [None, None]

    def sgn_prep(oc):
        if wtB[0] is None:
            wtB[0] = Buf("wt0", deps=woslabs[4:6])
            wtB[1] = Buf("wt1", deps=woslabs[6:8])
        sl = oc % 2
        bd.op("sync",
              lambda e, oc=oc, sl=sl:
              e.dma_start(wt_ap[sl], w_hbm["Wffn"].ap()[:, oc * 1024:(oc + 1) * 1024]),
              writes=[wtB[sl]], inc="dX", step=16)
        bd.op("vector",
              lambda e, oc=oc, sl=sl:
              e.tensor_scalar(A4.ap()[:, 8192 + oc * 1024: 8192 + (oc + 1) * 1024],
                              wt_ap[sl], 0.0, 0.5, ALU.is_ge, ALU.subtract),
              reads=[wtB[sl]], writes=[sgnB], inc="dv")

    xc56_ap = [f32v(A5, R1 + 4096 * i, 1024) for i in range(4)]
    u_ap = [f32v(A5, R3, 1024), f32v(A5, R3 + 4096, 1024)]
    uB = [rR3h[j].new(f"u{j}", 1)[0] for j in range(2)]
    g2 = rA3.new("g2", CH * 2)

    xc56 = [new_over(rR1e[2 * j:2 * j + 2], f"xc5_{j}") for j in range(4)]

    def xc_dma(xcb, o, h, sl):
        bd.op("sync",
              lambda e, o=o, h=h, sl=sl:
              e.dma_start(xc56_ap[sl], x_hbm.ap()[o * 128:(o + 1) * 128,
                                                  h * HF:(h + 1) * HF]),
              writes=[xcb[sl]], inc="dX", step=16)

    for pre in range(3):
        xc_dma(xc56, pre, 0, pre % 4)
    sgn_q = list(range(CH))
    for h in range(2):
        for q in range(2):
            os_ = [q * 4 + i for i in range(4)]
            ring_f = {}
            for o in os_:
                ring_f[o] = wo_ring_load("Wof", o)
            ring_b = {}
            for o in os_:
                ring_b[o] = wo_ring_load("Wob", o)
            for o in os_:
                idx = h * CH + o
                slot = idx % 4
                tiles, treads = [], []
                for k in range(CH):
                    for t in range(2):
                        tiles.append((wo_slab_ap[ring_f[o]][:, k * 128:(k + 1) * 128],
                                      A1.ap()[:, k * 2048 + h * HF + t * 512:
                                              k * 2048 + h * HF + (t + 1) * 512],
                                      ps_ap(slot, t * 512, (t + 1) * 512),
                                      k == 0, False))
                        treads.append([hf[k], woslabs[ring_f[o]]])
                bd.mm_group(tiles, treads, group_reads=(), writes=[PS[slot]])
            for o in os_:
                idx = h * CH + o
                sl = idx % 4
                slot = idx % 4
                if idx + 3 < 2 * CH:
                    no = (o + 3) % CH
                    nh = h if o + 3 < CH else h + 1
                    xc_dma(xc56, no, nh, (idx + 3) % 4)
                wld_slab_load()
                tiles, treads = [], []
                for k in range(CH):
                    for t in range(2):
                        tiles.append((wo_slab_ap[ring_b[o]][:, k * 128:(k + 1) * 128],
                                      A2.ap()[:, k * 2048 + h * HF + t * 512:
                                              k * 2048 + h * HF + (t + 1) * 512],
                                      ps_ap(slot, t * 512, (t + 1) * 512),
                                      False, k == CH - 1))
                        treads.append([hb[k], woslabs[ring_b[o]]])
                bd.mm_group(tiles, treads, group_reads=(), writes=[PS[slot]])
                bd.op("vector",
                      lambda e, slot=slot, sl=sl:
                      e.scalar_tensor_tensor(u_ap[sl % 2], ps_ap(slot, 0, 1024), 0.5,
                                             xc56_ap[sl], ALU.mult, ALU.add),
                      reads=[PS[slot], xc56[sl]], writes=[uB[sl % 2]], inc="dv")
                bd.op("scalar",
                      lambda e, o=o, h=h, sl=sl:
                      e.activation(A3.ap()[:, o * 2048 + h * HF: o * 2048 + (h + 1) * HF],
                                   u_ap[sl % 2], AF.Gelu_apprx_tanh,
                                   bias=dvec_sb[:, 16 + o:17 + o]),
                      reads=[uB[sl % 2], dvecB], writes=[g2[o * 2 + h]], inc="sc")

    rsq_ap = [A5.ap()[:, R4 // 2: R4 // 2 + 1024],
              A5.ap()[:, R4 // 2 + 1024: R4 // 2 + 2048]]
    rsqB = [Buf("rsq0"), Buf("rsq1")]
    out_ap = [f32v(A5, R5, 1024), f32v(A5, R5 + 4096, 1024)]
    outB = [rR5h[j].new(f"out{j}", 1)[0] for j in range(2)]
    srow = [srows.ap()[0:1, i * 1024:(i + 1) * 1024] for i in range(4)]
    srowB = [Buf(f"srow{i}") for i in range(4)]

    r1bf = A1.ap()
    x1bf = A2.ap()
    r2fv = [A3.ap().bitcast(F32), A1.ap().bitcast(F32)]

    def stats_tiles(src_ap_fn, src_buf, k, slot):
        tiles, treads = [], []
        for t in range(2):
            tiles.append((ones_sb.ap()[:],
                          src_ap_fn(k, t),
                          ps_ap(slot, t * 512, (t + 1) * 512)[0:1, :],
                          k == 0, k == CH - 1))
            treads.append([src_buf, onesB])
        bd.mm_group(tiles, treads, group_reads=(), writes=[PS[slot]])

    def ln_tail_ops(dst_write, tag, direct_psum=False):
        ops = []
        s1ps = ps_ap(2, 0, 1024)[0:1, :]
        s2ps = ps_ap(3, 0, 1024)[0:1, :]
        ops.append(lambda: bd.op("scalar",
                                 lambda e: e.activation(srow[2], s1ps, AF.Square,
                                                        scale=1.0 / float(np.sqrt(D))),
                                 reads=[PS[2]], writes=[srowB[2]], inc="sc"))
        ops.append(lambda: bd.op("scalar",
                                 lambda e: e.activation(srow[0], s1ps, AF.Identity,
                                                        scale=-1.0 / D),
                                 reads=[PS[2]], writes=[srowB[0]], inc="sc"))
        ops.append(lambda: bd.op("vector", lambda e: e.tensor_sub(srow[2], s2ps, srow[2]),
                                 reads=[PS[3], srowB[2]], writes=[srowB[2]], inc="dv"))
        ops.append(lambda: bd.op("scalar",
                                 lambda e: e.activation(srow[1], srow[2], AF.Ln,
                                                        bias=eps_ap, scale=1.0 / D),
                                 reads=[srowB[2], epsB], writes=[srowB[1]], inc="sc"))
        ops.append(lambda: bd.op("scalar",
                                 lambda e: e.activation(srow[3], srow[1], AF.Exp,
                                                        scale=-0.5),
                                 reads=[srowB[1]], writes=[srowB[3]], inc="sc"))
        hi = [srbf.ap()[0:1, 0:1024], srbf.ap()[0:1, 2048:3072]]
        lo = [srbf.ap()[0:1, 1024:2048], srbf.ap()[0:1, 3072:4096]]
        hiB = [Buf("hi0"), Buf("hi1")]
        loB = [Buf("lo0"), Buf("lo1")]

        def hilo(i, r):
            bd.op("scalar",
                  lambda e, i=i, r=r: e.activation(hi[i], srow[r], AF.Identity),
                  reads=[srowB[r]], writes=[hiB[i]], inc="sc")
            bd.op("vector",
                  lambda e, i=i, r=r: e.tensor_sub(lo[i], srow[r], hi[i]),
                  reads=[srowB[r], hiB[i]], writes=[loB[i]], inc="dv")

        if albe_B[0] is None:
            albe_B[0] = rR2.new("albe", 2)
        albe = albe_B[0]

        def bcast(i, slot):
            tiles, treads = [], []
            for part, pb, st in ((hi, hiB, True), (lo, loB, False)):
                for t in range(2):
                    tiles.append((onesb_sb.ap()[:],
                                  part[i][:, t * 512:(t + 1) * 512],
                                  ps_ap(slot, t * 512, (t + 1) * 512),
                                  st, not st))
                    treads.append([pb[i], onesB])
            bd.mm_group(tiles, treads, group_reads=(), writes=[PS[slot]])
            if not direct_psum:
                bd.op("scalar",
                      lambda e, i=i, slot=slot:
                      e.activation(albe_ap[i], ps_ap(slot, 0, 1024), AF.Identity),
                      reads=[PS[slot]], writes=[albe[i]], inc="sc")


        def bcast32(r, slot):
            tiles, treads = [], []
            for t in range(2):
                tiles.append((onesf_sb.ap()[:],
                              srow[r][:, t * 512:(t + 1) * 512],
                              ps_ap(slot, t * 512, (t + 1) * 512),
                              True, True))
                treads.append([srowB[r], onesB])
            bd.mm_group(tiles, treads, group_reads=(), writes=[PS[slot]])

        mulb = lambda: bd.op("vector",
                             lambda e: e.tensor_mul(srow[2], srow[0], srow[3]),
                             reads=[srowB[0], srowB[3]], writes=[srowB[2]], inc="dv")
        if direct_psum:
            ops.append(lambda: bcast32(3, 2))
            ops.append(mulb)
            ops.append(lambda: bcast32(2, 3))
        else:
            ops.append(lambda: hilo(0, 3))
            ops.append(lambda: bcast(0, 2))
            ops.append(mulb)
            ops.append(lambda: hilo(1, 2))
            ops.append(lambda: bcast(1, 3))
        ops += dst_write()
        return ops

    out_dma_count = [0]

    def bc_ap(slot, rep, width=1024):
        return bass.AP(ps_t[slot], ps_off[slot], [[2048, 128], [0, rep], [1, width]])

    albe_ap = [f32v(A5, R2, 1024), f32v(A5, R2 + 4096, 1024)]
    albe_B = [None]

    def albe_bc(i, rep):
        return albe_ap[i].unsqueeze(1).to_broadcast((128, rep, 1024))

    def make_ln1_apply(x1B, r1B, h):
        ops = []
        for pk in range(4):
            lo, hi_ = h * 8192 + pk * 2048, h * 8192 + (pk + 1) * 2048

            albe = albe_B[0]

            def op1(lo=lo, hi_=hi_, pk=pk, albe=albe):
                r1v3 = A1.ap()[:, lo:hi_].rearrange("p (c t) -> p c t", c=2)
                x1v3 = A2.ap()[:, lo:hi_].rearrange("p (c t) -> p c t", c=2)
                bd.op("vector",
                      lambda e: e.tensor_tensor(x1v3, r1v3, albe_bc(0, 2), ALU.mult),
                      reads=[r1B[2 * pk], r1B[2 * pk + 1], albe[0]],
                      writes=[x1B[2 * pk], x1B[2 * pk + 1]], inc="dv")

            def op2(lo=lo, hi_=hi_, pk=pk, albe=albe):
                x1v3 = A2.ap()[:, lo:hi_].rearrange("p (c t) -> p c t", c=2)
                bd.op("vector",
                      lambda e: e.tensor_tensor(x1v3, x1v3, albe_bc(1, 2), ALU.add),
                      reads=[albe[1], x1B[2 * pk], x1B[2 * pk + 1]],
                      writes=[x1B[2 * pk], x1B[2 * pk + 1]], inc="dv")
            ops += [op1, op2]
        return ops

    ln2u_ap = f32v(A5, R6, 2048)
    ln2u3 = ln2u_ap.rearrange("p (c t) -> p c t", c=2)
    ln2uB_holder = [None]

    def make_ln2_apply(r2B, h, direct_psum=False):
        r2f = r2fv[h]
        if ln2uB_holder[0] is None:
            ln2uB_holder[0] = Buf("ln2u", deps=[b for b in wtB if b is not None])
        u2B = ln2uB_holder[0]
        out3 = f32v(A5, R5, 2048).rearrange("p (c t) -> p c t", c=2)
        al_ap = bc_ap(2, 2) if direct_psum else albe_bc(0, 2)
        be_ap = bc_ap(3, 2) if direct_psum else albe_bc(1, 2)
        if direct_psum:
            a3f = A3.ap().bitcast(F32)
            o3B = [Buf(f"o3_{pk}", r2B_h0_holder[0]) for pk in range(4)]
        ops = []
        for pk in range(4):
            albe = albe_B[0]

            def op_mult(pk=pk, albe=albe):
                r2v3 = r2f[:, pk * 2048:(pk + 1) * 2048].rearrange("p (c t) -> p c t", c=2)
                ar = [PS[2]] if direct_psum else [albe[0]]
                dst = (a3f[:, pk * 2048:(pk + 1) * 2048].rearrange("p (c t) -> p c t", c=2)
                       if direct_psum else ln2u3)
                wr = [o3B[pk]] if direct_psum else [u2B]
                bd.op("vector",
                      lambda e, r2v3=r2v3, dst=dst: e.tensor_tensor(dst, r2v3, al_ap,
                                                                    ALU.mult),
                      reads=[r2B[2 * pk], r2B[2 * pk + 1]] + ar,
                      writes=wr, inc="dv")

            def op_add_store(pk=pk, albe=albe):
                br = [PS[3]] if direct_psum else [albe[1]]
                dst = (a3f[:, pk * 2048:(pk + 1) * 2048].rearrange("p (c t) -> p c t", c=2)
                       if direct_psum else out3)
                rd = [o3B[pk]] if direct_psum else [u2B]
                wr = [o3B[pk]] if direct_psum else [outB[0], outB[1]]
                src0 = dst if direct_psum else ln2u3
                bd.op("vector",
                      lambda e, dst=dst, src0=src0:
                      e.tensor_tensor(dst, src0, be_ap, ALU.add),
                      reads=rd + br, writes=wr, inc="dv")
                for j in range(2):
                    k = 2 * pk + j
                    if h == 1:
                        deng, q = (("scalar", "dO") if k % 2 == 0 else ("sync", "dX"))
                        srcap = a3f[:, k * 1024:(k + 1) * 1024]
                        rb = [o3B[pk]]
                    else:
                        deng, q = (("sync", "dX") if k % 2 == 0 else ("gpsimd", "dG"))
                        srcap = out_ap[j]
                        rb = [outB[j]]
                    bd.op(deng,
                          lambda e, k=k, srcap=srcap:
                          e.dma_start(out_hbm.ap()[k * 128:(k + 1) * 128,
                                                   h * HF:(h + 1) * HF],
                                      srcap),
                          reads=rb, inc=q, step=16)
                    out_dma_count[0] += 1
            ops += [op_mult, op_add_store]
        return ops

    bg = []

    def drain_bg(n):
        for _ in range(n):
            if bg:
                bg.pop(0)()

    def flush_bg():
        while bg:
            bg.pop(0)()


    rsqS6 = [Buf(f"rsq6_{k}", hb) for k in range(CH)]
    rsq6_ap = [x1bf[:, 8192 + k * 1024: 8192 + (k + 1) * 1024] for k in range(CH)]
    sq6_pend = []
    sq6_emitted = [0]

    def emit_sq6_until(j):
        while sq6_emitted[0] <= j and sq6_pend:
            sq6_pend.pop(0)()
            sq6_emitted[0] += 1

    r1B_h, r1view_h = {}, {}
    for h in range(2):
        r1B = rA1.new(f"r1h{h}", CH)
        r1B_h[h] = r1B
        r1view = r1bf[:, h * 8192: (h + 1) * 8192]
        r1view_h[h] = r1view
        xc6 = [new_over(rR1e[2 * j:2 * j + 2], f"xc6h{h}_{j}") for j in range(4)]
        for pre in range(3):
            xc_dma(xc6, pre, h, pre % 4)
        pending_stats = []
        for o in range(CH):
            if o + 3 < CH:
                xc_dma(xc6, o + 3, h, (o + 3) % 4)
            sl = o % 2
            slot = o % 2
            tiles, treads = [], []
            for k in range(CH):
                for t in range(2):
                    tiles.append((wtile(A4.ap()[:, 0:8192], o, k),
                                  A3.ap()[:, k * 2048 + h * HF + t * 512:
                                          k * 2048 + h * HF + (t + 1) * 512],
                                  ps_ap(slot, t * 512, (t + 1) * 512),
                                  k == 0, k == CH - 1))
                    treads.append([g2[k * 2 + h], WldS[o]])
            bd.mm_group(tiles, treads, group_reads=(), writes=[PS[slot]])
            if h == 0 and len(pending_stats) > 1:
                pending_stats.pop(0)()
            if h == 0 and sgn_q:
                sgn_prep(sgn_q.pop(0))
            drain_bg(4)
            xsl = o % 4
            bd.op("scalar",
                  lambda e, sl=sl, xsl=xsl: e.activation(u_ap[sl], xc56_ap[xsl],
                                                         AF.Copy, scale=2.0),
                  reads=[xc6[xsl]], writes=[uB[sl]], inc="sc")
            bd.op("vector",
                  lambda e, slot=slot, sl=sl, o=o, r1view=r1view:
                  e.scalar_tensor_tensor(r1view[:, o * 1024:(o + 1) * 1024],
                                         ps_ap(slot, 0, 1024), vcol("ldb", o),
                                         u_ap[sl], ALU.add, ALU.add),
                  reads=[PS[slot], uB[sl], vecsB], writes=[r1B[o]], inc="dv")
            if h == 0:
                bd.op("scalar",
                      lambda e, o=o, sl=sl, r1view=r1view:
                      e.activation(rsq_ap[sl], r1view[:, o * 1024:(o + 1) * 1024],
                                   AF.Square),
                      reads=[r1B[o]], writes=[rsqB[sl]], inc="sc")

                def _stats(o=o, sl=sl, r1view=r1view, r1B=r1B):
                    stats_tiles(lambda k, t, r1view=r1view:
                                r1view[:, k * 1024 + t * 512: k * 1024 + (t + 1) * 512],
                                r1B[o], o, 2)
                    stats_tiles(lambda k, t, sl=sl: rsq_ap[sl][:, t * 512:(t + 1) * 512],
                                rsqB[sl], o, 3)
                pending_stats.append(_stats)
            else:
                sq6_pend.append(lambda o=o, r1view=r1view, r1B=r1B: bd.op(
                    "scalar",
                    lambda e, o=o, r1view=r1view:
                    e.activation(rsq6_ap[o], r1view[:, o * 1024:(o + 1) * 1024],
                                 AF.Square),
                    reads=[r1B[o]], writes=[rsqS6[o]], inc="sc"))
                if o >= 2:
                    emit_sq6_until(o - 2)

                def _stats(o=o, r1view=r1view, r1B=r1B):
                    emit_sq6_until(o)
                    stats_tiles(lambda k, t, r1view=r1view:
                                r1view[:, k * 1024 + t * 512: k * 1024 + (t + 1) * 512],
                                r1B[o], o, 2)
                    stats_tiles(lambda k, t: rsq6_ap[k][:, t * 512:(t + 1) * 512],
                                rsqS6[o], o, 3)
                pending_stats.append(_stats)
                if o >= 4 and len(bg) <= 8:
                    while len(pending_stats) > (CH - 1 - o):
                        pending_stats.pop(0)()
        for f in pending_stats:
            f()
        if h == 0:
            x1B0 = rA2.new("x1h0", CH)

            def ln1a_write(x1B=x1B0, r1B=r1B, h=0):
                return make_ln1_apply(x1B, r1B, h)

            bg += ln_tail_ops(ln1a_write, "ln1h0")

    flush_bg()

    r1B = r1B_h[1]
    r1view = r1view_h[1]
    x1B1 = rA2.new("x1h1", CH)

    def ln1b_write(x1B=x1B1, r1B=r1B, h=1):
        return make_ln1_apply(x1B, r1B, h)

    bg += ln_tail_ops(ln1b_write, "ln1h1")

    x1B_h = {0: x1B0, 1: x1B1}
    r2B_h0_holder = [None]
    for h in range(2):
        x1B = x1B_h[h]
        x1view = x1bf[:, h * 8192:(h + 1) * 8192]
        r2f = r2fv[h]
        r2B = rA3.new("r2h0", CH) if h == 0 else rA1.new("r2h1", CH)
        if h == 0:
            r2B_h0_holder[0] = r2B
        rsqS = rA4lo.new(f"rsq_h{h}", CH)
        rsq_ap8 = [A4.ap()[:, k * 1024:(k + 1) * 1024] for k in range(CH)]
        sq_ops = []
        deferred2 = []
        r2bfB = [new_over(rR1e[k:k + 1], f"r2bfh{h}_{k}") for k in range(CH)]
        r2bf_ap8 = [A5.ap()[:, R1 // 2 + k * 1024: R1 // 2 + (k + 1) * 1024]
                    for k in range(CH)]
        deferred = []
        for o in range(CH):
            sl = o % 2
            slot = o % 2
            tiles, treads = [], []
            for k in range(CH):
                for t in range(2):
                    tiles.append((wtile(A4.ap()[:, 8192:16384], o, k),
                                  x1view[:, k * 1024 + t * 512: k * 1024 + (t + 1) * 512],
                                  ps_ap(slot, t * 512, (t + 1) * 512),
                                  k == 0, k == CH - 1))
                    treads.append([x1B[k], sgnB])
            bd.mm_group(tiles, treads, group_reads=(), writes=[PS[slot]])
            bd.op("scalar",
                  lambda e, slot=slot, sl=sl, o=o:
                  e.activation(u_ap[sl], ps_ap(slot, 0, 1024), AF.Relu,
                               bias=dvec_sb[:, 32 + o:33 + o],
                               scale=dvec_sb[:, 24 + o:25 + o]),
                  reads=[PS[slot], dvecB], writes=[uB[sl]], inc="sc")
            bd.op("scalar",
                  lambda e, sl=sl:
                  e.activation(u_ap[sl], u_ap[sl], AF.Relu,
                               bias=dvec_sb[:, 40:41], scale=-1.0),
                  reads=[uB[sl], dvecB], writes=[uB[sl]], inc="sc")
            bd.op("vector",
                  lambda e, sl=sl, o=o, x1view=x1view, r2f=r2f:
                  e.scalar_tensor_tensor(r2f[:, o * 1024:(o + 1) * 1024],
                                         u_ap[sl], -1.0,
                                         x1view[:, o * 1024:(o + 1) * 1024],
                                         ALU.mult, ALU.add),
                  reads=[uB[sl], x1B[o]], writes=[r2B[o]], inc="dv")
            bd.op("vector",
                  lambda e, o=o, r2f=r2f: e.tensor_copy(r2bf_ap8[o],
                                               r2f[:, o * 1024:(o + 1) * 1024]),
                  reads=[r2B[o]], writes=[r2bfB[o]], inc="dv")
            sq_ops.append(lambda o=o: bd.op(
                "scalar",
                lambda e, o=o: e.activation(rsq_ap8[o], r2bf_ap8[o], AF.Square),
                reads=[r2bfB[o]], writes=[rsqS[o]], inc="sc"))
            if o >= 2:
                sq_ops.pop(0)()
            while len(bg) > max(0, 6 - 2 * (o // 2)):
                bg.pop(0)()
            if not bg and deferred:
                deferred.pop(0)()

            def _sum(o=o):
                stats_tiles(lambda k, t: r2bf_ap8[k][:, t * 512:(t + 1) * 512],
                            r2bfB[o], o, 2)

            def _sq(o=o):
                stats_tiles(lambda k, t: rsq_ap8[k][:, t * 512:(t + 1) * 512],
                            rsqS[o], o, 3)
            deferred.append(_sum)
            deferred2.append(_sq)
        flush_bg()
        while sq_ops:
            sq_ops.pop(0)()
        for f in deferred:
            f()
        for f in deferred2:
            f()

        def ln2_write(r2B=r2B, h=h):
            return make_ln2_apply(r2B, h, direct_psum=(h == 1))

        bg += ln_tail_ops(ln2_write, f"ln2h{h}", direct_psum=(h == 1))

    flush_bg()

    def final_wait(e):
        for base in ("dO", "dX", "dG"):
            for j in range(Builder.DMA_POOL):
                nm = f"{base}{j}"
                if bd.counts.get(nm, 0):
                    e.wait_ge(bd.sems[nm], bd.counts[nm])
    bd.raw("scalar", final_wait)

    with nc.Block() as block:
        @block.tensor
        def _(e):
            for th in bd.q["tensor"]:
                th(e)

        @block.vector
        def _(e):
            for th in bd.q["vector"]:
                th(e)

        @block.scalar
        def _(e):
            for th in bd.q["scalar"]:
                th(e)

        @block.sync
        def _(e):
            for th in bd.q["sync"]:
                th(e)

        @block.gpsimd
        def _(e):
            for th in bd.q["gpsimd"]:
                th(e)

    nc.finalize()
    return nc


_NC_CACHE = {}


def _blocked(w):
    return np.ascontiguousarray(
        w.reshape(8, 128, 8, 128).transpose(1, 2, 0, 3).reshape(128, 8192))


def kernel(L0_attn_x, L0_attn_fwd_a_logit, L0_attn_fwd_Wx, L0_attn_fwd_bx,
           L0_attn_fwd_Wgx, L0_attn_fwd_bgx, L0_attn_fwd_Wga, L0_attn_fwd_bga,
           L0_attn_fwd_Wo, L0_attn_fwd_bo, L0_attn_bwd_a_logit, L0_attn_bwd_Wx,
           L0_attn_bwd_bx, L0_attn_bwd_Wgx, L0_attn_bwd_bgx, L0_attn_bwd_Wga,
           L0_attn_bwd_bga, L0_attn_bwd_Wo, L0_attn_bwd_bo, L0_attn_ld_W,
           L0_attn_ld_b, L0_ffn_weights, L0_ffn_scale, L0_ffn_bias):
    x = np.asarray(L0_attn_x, dtype=np.float32)
    assert x.shape == (B, S, D)

    if "nc" not in _NC_CACHE:
        _NC_CACHE["nc"] = build_program()
    nc = _NC_CACHE["nc"]

    wmap = {
        "Wxf": _blocked(np.asarray(L0_attn_fwd_Wx, np.float32)),
        "Wxb": _blocked(np.asarray(L0_attn_bwd_Wx, np.float32)),
        "Wgxf": _blocked(np.asarray(L0_attn_fwd_Wgx, np.float32)),
        "Wgxb": _blocked(np.asarray(L0_attn_bwd_Wgx, np.float32)),
        "Wgaf": _blocked(np.asarray(L0_attn_fwd_Wga, np.float32)),
        "Wgab": _blocked(np.asarray(L0_attn_bwd_Wga, np.float32)),
        "Wof": _blocked(np.asarray(L0_attn_fwd_Wo, np.float32)),
        "Wob": _blocked(np.asarray(L0_attn_bwd_Wo, np.float32)),
        "Wld": _blocked(np.asarray(L0_attn_ld_W, np.float32)),
        "Wffn": _blocked(np.ascontiguousarray(np.asarray(L0_ffn_weights, np.float32).T)),
    }
    vecs = np.zeros((128, 8 * len(VEC_NAMES)), np.float32)
    vv = {
        "af": L0_attn_fwd_a_logit, "ab": L0_attn_bwd_a_logit,
        "bxf": L0_attn_fwd_bx, "bxb": L0_attn_bwd_bx,
        "bgxf": L0_attn_fwd_bgx, "bgxb": L0_attn_bwd_bgx,
        "bgaf": L0_attn_fwd_bga, "bgab": L0_attn_bwd_bga,
        "bof": L0_attn_fwd_bo, "bob": L0_attn_bwd_bo,
        "ldb": L0_attn_ld_b, "fsc": L0_ffn_scale, "fbi": L0_ffn_bias,
    }
    for nm, v in vv.items():
        vecs[:, VCOL[nm]:VCOL[nm] + 8] = np.asarray(v, np.float32).reshape(8, 128).T

    in_maps = []
    for b in range(B):
        m = {"x": np.ascontiguousarray(x[b].T), "vecs": vecs}
        m.update(wmap)
        in_maps.append(m)

    res = run_bass_kernel_spmd(nc, in_maps, core_ids=list(range(B)))
    out = np.empty((B, S, D), np.float32)
    for b in range(B):
        out[b] = np.asarray(res.results[b]["out"]).T
    return out


# version 39 (final)
# speedup vs baseline: 1.0424x; 1.0424x over previous
import numpy as np
import concourse.bass as bass
import concourse.mybir as mybir
from concourse.bass_utils import run_bass_kernel_spmd

F32 = mybir.dt.float32
BF16 = mybir.dt.bfloat16
AF = mybir.ActivationFunctionType
ALU = mybir.AluOpType

D = 1024
S = 2048
B = 8
CH = 8
HF = S // 2
NT = 4
EPS = 1e-5

ENGINES = ("tensor", "vector", "scalar", "sync", "gpsimd")

VEC_NAMES = ["af", "ab", "bxf", "bxb", "bgxf", "bgxb", "bgaf", "bgab",
             "bof", "bob", "ldb", "fsc", "fbi"]
VCOL = {n: i * 8 for i, n in enumerate(VEC_NAMES)}


class Buf:
    __slots__ = ("name", "writers", "readers")

    def __init__(self, name, deps=()):
        self.name = name
        self.writers = []
        self.readers = []
        for d in deps:
            self.writers = self.writers + d.writers + d.readers

    def wrote(self, tok):
        self.writers = [tok]
        self.readers = []

    def read(self, tok):
        if tok is not None:
            self.readers.append(tok)


class Region:

    def __init__(self, name):
        self.name = name
        self.cur = []

    def new(self, prefix, n=1):
        bs = [Buf(f"{prefix}{i}", self.cur) for i in range(n)]
        self.cur = bs
        return bs


class Builder:
    DMA_POOL = 8

    def __init__(self, nc):
        self.nc = nc
        self.q = {e: [] for e in ENGINES}
        self.sems = {}
        self.counts = {}
        self.waited = {e: {} for e in ENGINES}

    def sem(self, name):
        if name not in self.sems:
            self.sems[name] = self.nc.alloc_semaphore(name)
            self.counts[name] = 0
        return self.sems[name]

    def _need(self, reads, writes):
        need = {}
        for b in reads:
            for (s, v) in b.writers:
                if v > need.get(s, 0):
                    need[s] = v
        for b in writes:
            for (s, v) in b.writers + b.readers:
                if v > need.get(s, 0):
                    need[s] = v
        return need

    def _filter(self, eng, need):
        wl = []
        for s, v in sorted(need.items()):
            if eng == "tensor" and s == "pe":
                continue
            if self.waited[eng].get(s, 0) < v:
                wl.append((s, v))
                self.waited[eng][s] = v
        return wl

    def op(self, eng, emit, reads=(), writes=(), inc=None, step=1):
        if inc in ("dX", "dW", "dO"):
            k = f"_{inc}_rr"
            j = self.counts.get(k, 0)
            self.counts[k] = j + 1
            inc = f"{inc}{j % self.DMA_POOL}"
            prev = self.counts.get(inc, 0)
            if prev and self.waited[eng].get(inc, 0) < prev:
                pb = Buf(f"_rr_{inc}_{prev}")
                pb.writers = [(inc, prev)]
                reads = list(reads) + [pb]
        return self.group(eng, [emit], reads=reads, writes=writes, inc=inc,
                          step=step)

    def group(self, eng, emits, reads=(), writes=(), inc=None, step=1):
        wl = self._filter(eng, self._need(reads, writes))
        tok = None
        if inc is not None:
            self.sem(inc)
            self.counts[inc] += step
            tok = (inc, self.counts[inc])
        sems = self.sems

        def thunk(e, wl=tuple(wl), emits=tuple(emits), tok=tok, step=step):
            for s, v in wl:
                e.wait_ge(sems[s], v)
            inst = None
            for em in emits:
                inst = em(e)
            if tok is not None:
                inst.then_inc(sems[tok[0]], step)

        self.q[eng].append(thunk)
        for b in reads:
            b.read(tok)
        for b in writes:
            b.wrote(tok)
        return tok

    def mm_group(self, tiles, tile_reads, group_reads=(), writes=(), inc="pe"):
        head = self._filter("tensor", self._need(group_reads, writes))
        per = []
        for trs in tile_reads:
            per.append(tuple(self._filter("tensor", self._need(trs, ()))))
        self.sem(inc)
        self.counts[inc] += 1
        tok = (inc, self.counts[inc])
        sems = self.sems

        def thunk(e, head=tuple(head), per=tuple(per), tiles=tuple(tiles),
                  tok=tok):
            for s, v in head:
                e.wait_ge(sems[s], v)
            inst = None
            for (lh, rh, oa, st, sp), wl in zip(tiles, per):
                for s, v in wl:
                    e.wait_ge(sems[s], v)
                inst = e.matmul(oa, lh, rh, start=st, stop=sp)
            inst.then_inc(sems[tok[0]], 1)

        self.q["tensor"].append(thunk)
        seen = set()
        for trs in tile_reads:
            for b in trs:
                if id(b) not in seen:
                    seen.add(id(b))
                    b.read(tok)
        for b in group_reads:
            b.read(tok)
        for b in writes:
            b.wrote(tok)
        return tok

    def raw(self, eng, emit):
        self.q[eng].append(lambda e, emit=emit: emit(e))


def build_program():
    nc = bass.Bass("TRN2", target_bir_lowering=False)

    x_hbm = nc.dram_tensor("x", [D, S], F32, kind="ExternalInput")
    w_hbm = {}
    for wn in ["Wxf", "Wxb", "Wgxf", "Wgxb", "Wgaf", "Wgab", "Wof", "Wob",
               "Wld", "Wffn"]:
        w_hbm[wn] = nc.dram_tensor(wn, [128, CH * D], F32, kind="ExternalInput")
    vecs_hbm = nc.dram_tensor("vecs", [128, 8 * len(VEC_NAMES)], F32,
                              kind="ExternalInput")
    out_hbm = nc.dram_tensor("out", [D, S], F32, kind="ExternalOutput")

    A1 = nc.alloc_sbuf_tensor("A1", [128, 16384], BF16)
    A2 = nc.alloc_sbuf_tensor("A2", [128, 16384], BF16)
    A3 = nc.alloc_sbuf_tensor("A3", [128, 16384], BF16)
    A4 = nc.alloc_sbuf_tensor("A4", [128, 16384], BF16)
    A5 = nc.alloc_sbuf_tensor("A5", [128, 26624], BF16)
    vecs_sb = nc.alloc_sbuf_tensor("vecs_sb", [128, 8 * len(VEC_NAMES)], F32)
    dvec_sb = nc.alloc_sbuf_tensor("dvec_sb", [128, 48], F32)
    ones_sb = nc.alloc_sbuf_tensor("ones_sb", [128, 1], BF16)
    onesf_sb = nc.alloc_sbuf_tensor("onesf_sb", [1, 128], F32)
    onesb_sb = nc.alloc_sbuf_tensor("onesb_sb", [1, 128], BF16)
    srbf = nc.alloc_sbuf_tensor("srbf", [1, 4096], BF16)
    srows = nc.alloc_sbuf_tensor("srows", [1, 4112], F32)

    psA = nc.alloc_psum_tensor("psA", [128, 2048], F32)
    psB = nc.alloc_psum_tensor("psB", [128, 2048], F32)

    def f32v(arena, byte_off, n):
        return arena.ap()[:, byte_off // 2: byte_off // 2 + 2 * n].bitcast(F32)

    R1 = 0
    R2 = 16384
    R3 = 24576
    R4 = 32768
    R5 = 36864
    R6 = 45056

    bd = Builder(nc)
    for base in ("dW", "dX", "dO"):
        for j in range(Builder.DMA_POOL):
            bd.sem(f"{base}{j}")
    for sname in ("pe", "dv", "sc", "gp"):
        bd.sem(sname)

    rA1 = Region("A1")
    rA2 = Region("A2")
    rA3 = Region("A3")
    rA4lo = Region("A4lo")
    rA4hi = Region("A4hi")
    rR1e = [Region(f"R1e{j}") for j in range(8)]
    rR2 = Region("R2")
    rR3h = [Region(f"R3h{j}") for j in range(2)]
    rR5h = [Region(f"R5h{j}") for j in range(2)]

    def new_over(regions, prefix):
        deps = [b for r in regions for b in r.cur]
        b = Buf(prefix, deps)
        for r in regions:
            r.cur = [b]
        return b

    vecsB = Buf("vecs")
    dvecB = Buf("dvec")
    onesB = Buf("ones")
    PS = [Buf("ps0"), Buf("ps1"), Buf("ps2"), Buf("ps3")]
    ps_t = {0: psA, 1: psA, 2: psB, 3: psB}
    ps_off = {0: 0, 1: 1024, 2: 0, 3: 1024}

    def ps_ap(slot, lo, hi):
        return ps_t[slot].ap()[:, ps_off[slot] + lo: ps_off[slot] + hi]

    def vcol(nm, o):
        return vecs_sb[:, VCOL[nm] + o: VCOL[nm] + o + 1]

    def ae_col(d, o):
        base = 0 if d == 0 else 8
        return dvec_sb[:, base + o: base + o + 1]

    bd.op("sync", lambda e: e.dma_start(vecs_sb[:], vecs_hbm.ap()),
          writes=[vecsB], inc="dX", step=16)
    eps_ap = srows.ap()[0:1, 4096:4097]
    epsB = Buf("eps")
    bd.op("vector", lambda e: e.memset(eps_ap, EPS), writes=[epsB], inc="dv")

    WxfS = rA4lo.new("Wxf", CH)
    WxbS = rA4hi.new("Wxb", CH)
    for oc in range(CH):
        bd.op("gpsimd",
              lambda e, oc=oc:
              e.dma_start(A4.ap()[:, oc * 1024:(oc + 1) * 1024],
                          w_hbm["Wxf"].ap()[:, oc * 1024:(oc + 1) * 1024]),
              writes=[WxfS[oc]], inc="dW", step=16)

    def _wxb():
        for oc in range(CH):
            bd.op("gpsimd",
                  lambda e, oc=oc:
                  e.dma_start(A4.ap()[:, 8192 + oc * 1024: 8192 + (oc + 1) * 1024],
                              w_hbm["Wxb"].ap()[:, oc * 1024:(oc + 1) * 1024]),
                  writes=[WxbS[oc]], inc="dW", step=16)
    wxb_load = [_wxb]

    xgH = [rA1.new(f"xg{c}", 2) for c in range(CH)]
    xcP1 = ([new_over(rR1e[2 * j:2 * j + 2], f"xcP1_{j}") for j in range(4)] +
            [rR3h[j].new(f"xcP1b{j}", 1)[0] for j in range(2)] +
            [rR5h[j].new(f"xcP1c{j}", 1)[0] for j in range(2)])
    xcP1_ap = ([f32v(A5, R1 + 4096 * i, 1024) for i in range(4)] +
               [f32v(A5, R3, 1024), f32v(A5, R3 + 4096, 1024),
                f32v(A5, R5, 1024), f32v(A5, R5 + 4096, 1024)])

    def p1_dma(i):
        c, hh = i // 2, i % 2
        sl = i % 8
        eng, q = (("sync", "dX") if i % 2 == 0 else ("scalar", "dO"))
        bd.op(eng,
              lambda e, c=c, hh=hh, sl=sl:
              e.dma_start(xcP1_ap[sl],
                          x_hbm.ap()[c * 128:(c + 1) * 128,
                                     hh * 1024:(hh + 1) * 1024]),
              writes=[xcP1[sl]], inc=q, step=16)

    for i in range(8):
        p1_dma(i)
    bd.op("scalar", lambda e: e.activation(dvec_sb[:, 0:8],
                                           vecs_sb[:, VCOL["af"]:VCOL["af"] + 8],
                                           AF.Sigmoid),
          reads=[vecsB], writes=[dvecB], inc="sc")
    bd.op("scalar", lambda e: e.activation(dvec_sb[:, 8:16],
                                           vecs_sb[:, VCOL["ab"]:VCOL["ab"] + 8],
                                           AF.Sigmoid),
          reads=[vecsB], writes=[dvecB], inc="sc")
    bd.op("vector", lambda e: e.tensor_add(dvec_sb[:, 16:24],
                                           vecs_sb[:, VCOL["bof"]:VCOL["bof"] + 8],
                                           vecs_sb[:, VCOL["bob"]:VCOL["bob"] + 8]),
          reads=[vecsB], writes=[dvecB], inc="dv")
    bd.op("vector", lambda e: e.tensor_scalar_mul(dvec_sb[:, 16:24],
                                                  dvec_sb[:, 16:24], 0.5),
          reads=[dvecB], writes=[dvecB], inc="dv")
    bd.op("vector", lambda e: e.tensor_scalar_mul(dvec_sb[:, 24:32],
                                                  vecs_sb[:, VCOL["fsc"]:VCOL["fsc"] + 8],
                                                  2.0),
          reads=[vecsB], writes=[dvecB], inc="dv")
    bd.op("vector", lambda e: e.tensor_scalar_add(dvec_sb[:, 32:40],
                                                  vecs_sb[:, VCOL["fbi"]:VCOL["fbi"] + 8],
                                                  100.0),
          reads=[vecsB], writes=[dvecB], inc="dv")
    bd.op("vector", lambda e: e.memset(dvec_sb[:, 40:41], 200.0),
          writes=[dvecB], inc="dv")
    bd.op("vector", lambda e: e.memset(ones_sb[:], 1.0), writes=[onesB], inc="dv")
    bd.op("vector", lambda e: e.memset(onesf_sb[:], 1.0), writes=[onesB], inc="dv")
    bd.op("vector", lambda e: e.memset(onesb_sb[:], 1.0), writes=[onesB], inc="dv")
    for i in range(2 * CH):
        c, hh = i // 2, i % 2
        sl = i % 8
        bd.op("scalar",
              lambda e, c=c, hh=hh, sl=sl:
              e.activation(A1.ap()[:, c * 2048 + hh * 1024: c * 2048 + (hh + 1) * 1024],
                           xcP1_ap[sl], AF.Gelu_apprx_tanh),
              reads=[xcP1[sl]], writes=[xgH[c][hh]], inc="sc")
        if i + 8 < 2 * CH:
            p1_dma(i + 8)
        if i == 7 and wxb_load:
            wxb_load.pop(0)()

    def wtile(arena_ap, o, k):
        return arena_ap[:, o * 1024 + k * 128: o * 1024 + (k + 1) * 128]

    def p2_direction(wview, wslabs, bias_name, xb_arena, xb_bufs):
        for o in range(CH):
            slots = (0, 1) if o % 2 == 0 else (2, 3)
            pst = psA if o % 2 == 0 else psB
            tiles, treads = [], []
            for k in range(CH):
                for t in range(NT):
                    tiles.append((wtile(wview, o, k),
                                  A1.ap()[:, k * 2048 + t * 512: k * 2048 + (t + 1) * 512],
                                  pst.ap()[:, t * 512:(t + 1) * 512],
                                  k == 0, k == CH - 1))
                    treads.append([xgH[k][t // 2], wslabs[o]])
            bd.mm_group(tiles, treads, group_reads=[],
                        writes=[PS[slots[0]], PS[slots[1]]])
            bd.op("scalar",
                  lambda e, o=o, pst=pst, bias_name=bias_name, xb_arena=xb_arena:
                  e.activation(xb_arena.ap()[:, o * 2048:(o + 1) * 2048], pst.ap()[:],
                               AF.Identity, bias=vcol(bias_name, o)),
                  reads=[PS[slots[0]], PS[slots[1]], vecsB],
                  writes=[xb_bufs[o]], inc="sc")

    xbf = rA2.new("xbf", CH)
    xbb = rA3.new("xbb", CH)
    p2_direction(A4.ap()[:, 0:8192], WxfS, "bxf", A2, xbf)
    p2_direction(A4.ap()[:, 8192:16384], WxbS, "bxb", A3, xbb)

    A4f = A4.ap().bitcast(F32)
    gx_ap = [A4f[:, 0:2048], f32v(A5, R1, 2048)]
    ga_ap = [A4f[:, 2048:4096], f32v(A5, R1 + 8192, 2048)]
    ab_ap = [A4f[:, 4096:6144], f32v(A5, R3, 2048)]
    tb_ap = [A4f[:, 6144:8192], f32v(A5, R5, 2048)]
    slab_ap = [A5.ap()[:, 8192 + i * 1024: 8192 + (i + 1) * 1024] for i in range(4)]

    scr0 = rA4lo.new("scr0", 2) + rA4hi.new("scr0h", 2)
    scr1 = [new_over(rR1e[0:4], "scr1gx"), new_over(rR1e[4:8], "scr1ga"),
            new_over(rR3h, "scr1ab"), new_over(rR5h, "scr1tb")]
    gxB = [scr0[0], scr1[0]]
    gaB = [scr0[1], scr1[1]]
    abB = [scr0[2], scr1[2]]
    tbB = [scr0[3], scr1[3]]
    slabs = rR2.new("slab", 4)

    def gates_direction(d, wgx_name, wga_name, bgx_name, bga_name, xb_arena,
                        xb_bufs, h_arena, h_bufs):
        def slab_dma(which, o):
            i = (0 if which == 0 else 2) + o % 2
            nm = wgx_name if which == 0 else wga_name
            bd.op("gpsimd",
                  lambda e, i=i, nm=nm, o=o:
                  e.dma_start(slab_ap[i], w_hbm[nm].ap()[:, o * 1024:(o + 1) * 1024]),
                  writes=[slabs[i]], inc="dW", step=16)

        slab_dma(0, 0)
        slab_dma(1, 0)
        for o in range(CH):
            if o + 1 < CH:
                slab_dma(0, o + 1)
                slab_dma(1, o + 1)
            p = o % 2
            igx = 0 + p
            iga = 2 + p
            tiles, treads = [], []
            for k in range(CH):
                for t in range(NT):
                    tiles.append((slab_ap[igx][:, k * 128:(k + 1) * 128],
                                  xb_arena.ap()[:, k * 2048 + t * 512: k * 2048 + (t + 1) * 512],
                                  psA.ap()[:, t * 512:(t + 1) * 512], k == 0, k == CH - 1))
                    treads.append([xb_bufs[k]])
            bd.mm_group(tiles, treads, group_reads=[slabs[igx]],
                        writes=[PS[0], PS[1]])
            tiles, treads = [], []
            for k in range(CH):
                for t in range(NT):
                    tiles.append((slab_ap[iga][:, k * 128:(k + 1) * 128],
                                  xb_arena.ap()[:, k * 2048 + t * 512: k * 2048 + (t + 1) * 512],
                                  psB.ap()[:, t * 512:(t + 1) * 512], k == 0, k == CH - 1))
                    treads.append([xb_bufs[k]])
            bd.mm_group(tiles, treads, group_reads=[slabs[iga]],
                        writes=[PS[2], PS[3]])
            bd.op("scalar",
                  lambda e, o=o, p=p, bgx_name=bgx_name:
                  e.activation(gx_ap[p], psA.ap()[:], AF.Sigmoid, bias=vcol(bgx_name, o)),
                  reads=[PS[0], PS[1], vecsB], writes=[gxB[p]], inc="sc")
            bd.op("scalar",
                  lambda e, o=o, p=p, bga_name=bga_name:
                  e.activation(ga_ap[p], psB.ap()[:], AF.Sigmoid, bias=vcol(bga_name, o)),
                  reads=[PS[2], PS[3], vecsB], writes=[gaB[p]], inc="sc")
            bd.op("vector",
                  lambda e, o=o, d=d, p=p:
                  e.tensor_single_scalar(ab_ap[p], ga_ap[p], ae_col(d, o), ALU.mult),
                  reads=[gaB[p], dvecB], writes=[abB[p]], inc="dv")
            bd.op("scalar", lambda e, p=p: e.activation(tb_ap[p], ab_ap[p], AF.Square),
                  reads=[abB[p]], writes=[tbB[p]], inc="sc")
            bd.op("scalar", lambda e, p=p: e.activation(tb_ap[p], tb_ap[p], AF.Sqrt,
                                                        bias=1.0, scale=-1.0),
                  reads=[tbB[p]], writes=[tbB[p]], inc="sc")
            bd.op("vector",
                  lambda e, o=o, p=p, xb_arena=xb_arena:
                  e.tensor_mul(gx_ap[p], gx_ap[p], xb_arena.ap()[:, o * 2048:(o + 1) * 2048]),
                  reads=[gxB[p], xb_bufs[o]], writes=[gxB[p]], inc="dv")
            bd.op("vector", lambda e, p=p: e.tensor_mul(gx_ap[p], gx_ap[p], tb_ap[p]),
                  reads=[gxB[p], tbB[p]], writes=[gxB[p]], inc="dv")
            hview = h_arena.ap()[:, o * 2048:(o + 1) * 2048]
            if d == 0:
                bd.op("vector",
                      lambda e, hview=hview, p=p:
                      e.tensor_tensor_scan(hview, ab_ap[p], gx_ap[p], 0.0,
                                           ALU.mult, ALU.add),
                      reads=[abB[p], gxB[p]], writes=[h_bufs[o]], inc="dv")
            else:
                bd.op("vector",
                      lambda e, hview=hview, p=p:
                      e.tensor_tensor_scan(hview[:, ::-1], ab_ap[p][:, ::-1],
                                           gx_ap[p][:, ::-1], 0.0, ALU.mult, ALU.add),
                      reads=[abB[p], gxB[p]], writes=[h_bufs[o]], inc="dv")

    rA1.cur = [b for pair in xgH for b in pair]
    hf = rA1.new("hf", CH)
    gates_direction(0, "Wgxf", "Wgaf", "bgxf", "bgaf", A2, xbf, A1, hf)
    hb = rA2.new("hb", CH)
    gates_direction(1, "Wgxb", "Wgab", "bgxb", "bgab", A3, xbb, A2, hb)

    WldS = rA4lo.new("Wld", CH)
    wld_q = list(range(CH))

    def wld_slab_load():
        if not wld_q:
            return
        oc = wld_q.pop(0)
        bd.op("gpsimd",
              lambda e, oc=oc:
              e.dma_start(A4.ap()[:, oc * 1024:(oc + 1) * 1024],
                          w_hbm["Wld"].ap()[:, oc * 1024:(oc + 1) * 1024]),
              writes=[WldS[oc]], inc="dW", step=16)
    sgnB = rA4hi.new("sgnW")[0]

    wo_slab_ap = slab_ap + [A5.ap()[:, R6 // 2 + i * 1024: R6 // 2 + (i + 1) * 1024]
                            for i in range(4)]
    woslabs = rR2.new("woslab", 4) + [Buf(f"woslabx{i}") for i in range(4)]
    wo_seq = [4]

    def wo_ring_load(nm, oc):
        i = wo_seq[0] % 8
        wo_seq[0] += 1
        bd.op("gpsimd",
              lambda e, i=i, nm=nm, oc=oc:
              e.dma_start(wo_slab_ap[i], w_hbm[nm].ap()[:, oc * 1024:(oc + 1) * 1024]),
              writes=[woslabs[i]], inc="dW", step=16)
        return i

    wt_ap = [f32v(A5, R6, 1024), f32v(A5, R6 + 4096, 1024)]
    wtB = [None, None]

    def sgn_prep(oc):
        if wtB[0] is None:
            wtB[0] = Buf("wt0", deps=woslabs[4:6])
            wtB[1] = Buf("wt1", deps=woslabs[6:8])
        sl = oc % 2
        bd.op("sync",
              lambda e, oc=oc, sl=sl:
              e.dma_start(wt_ap[sl], w_hbm["Wffn"].ap()[:, oc * 1024:(oc + 1) * 1024]),
              writes=[wtB[sl]], inc="dX", step=16)
        bd.op("vector",
              lambda e, oc=oc, sl=sl:
              e.tensor_scalar(A4.ap()[:, 8192 + oc * 1024: 8192 + (oc + 1) * 1024],
                              wt_ap[sl], 0.0, 0.5, ALU.is_ge, ALU.subtract),
              reads=[wtB[sl]], writes=[sgnB], inc="dv")

    xc56_ap = [f32v(A5, R1 + 4096 * i, 1024) for i in range(4)]
    u_ap = [f32v(A5, R3, 1024), f32v(A5, R3 + 4096, 1024)]
    uB = [rR3h[j].new(f"u{j}", 1)[0] for j in range(2)]
    g2 = rA3.new("g2", CH * 2)

    xc56 = [new_over(rR1e[2 * j:2 * j + 2], f"xc5_{j}") for j in range(4)]

    def xc_dma(xcb, o, h, sl):
        bd.op("sync",
              lambda e, o=o, h=h, sl=sl:
              e.dma_start(xc56_ap[sl], x_hbm.ap()[o * 128:(o + 1) * 128,
                                                  h * HF:(h + 1) * HF]),
              writes=[xcb[sl]], inc="dX", step=16)

    for pre in range(3):
        xc_dma(xc56, pre, 0, pre % 4)
    sgn_q = list(range(CH))
    for h in range(2):
        for q in range(2):
            os_ = [q * 4 + i for i in range(4)]
            ring_f = {}
            for o in os_:
                ring_f[o] = wo_ring_load("Wof", o)
            ring_b = {}
            for o in os_:
                ring_b[o] = wo_ring_load("Wob", o)
            for o in os_:
                idx = h * CH + o
                slot = idx % 4
                tiles, treads = [], []
                for k in range(CH):
                    for t in range(2):
                        tiles.append((wo_slab_ap[ring_f[o]][:, k * 128:(k + 1) * 128],
                                      A1.ap()[:, k * 2048 + h * HF + t * 512:
                                              k * 2048 + h * HF + (t + 1) * 512],
                                      ps_ap(slot, t * 512, (t + 1) * 512),
                                      k == 0, False))
                        treads.append([hf[k], woslabs[ring_f[o]]])
                bd.mm_group(tiles, treads, group_reads=(), writes=[PS[slot]])
            for o in os_:
                idx = h * CH + o
                sl = idx % 4
                slot = idx % 4
                if idx + 3 < 2 * CH:
                    no = (o + 3) % CH
                    nh = h if o + 3 < CH else h + 1
                    xc_dma(xc56, no, nh, (idx + 3) % 4)
                wld_slab_load()
                tiles, treads = [], []
                for k in range(CH):
                    for t in range(2):
                        tiles.append((wo_slab_ap[ring_b[o]][:, k * 128:(k + 1) * 128],
                                      A2.ap()[:, k * 2048 + h * HF + t * 512:
                                              k * 2048 + h * HF + (t + 1) * 512],
                                      ps_ap(slot, t * 512, (t + 1) * 512),
                                      False, k == CH - 1))
                        treads.append([hb[k], woslabs[ring_b[o]]])
                bd.mm_group(tiles, treads, group_reads=(), writes=[PS[slot]])
                bd.op("vector",
                      lambda e, slot=slot, sl=sl:
                      e.scalar_tensor_tensor(u_ap[sl % 2], ps_ap(slot, 0, 1024), 0.5,
                                             xc56_ap[sl], ALU.mult, ALU.add),
                      reads=[PS[slot], xc56[sl]], writes=[uB[sl % 2]], inc="dv")
                bd.op("scalar",
                      lambda e, o=o, h=h, sl=sl:
                      e.activation(A3.ap()[:, o * 2048 + h * HF: o * 2048 + (h + 1) * HF],
                                   u_ap[sl % 2], AF.Gelu_apprx_tanh,
                                   bias=dvec_sb[:, 16 + o:17 + o]),
                      reads=[uB[sl % 2], dvecB], writes=[g2[o * 2 + h]], inc="sc")

    rsq_ap = [A5.ap()[:, R4 // 2: R4 // 2 + 1024],
              A5.ap()[:, R4 // 2 + 1024: R4 // 2 + 2048]]
    rsqB = [Buf("rsq0"), Buf("rsq1")]
    out_ap = [f32v(A5, R5, 1024), f32v(A5, R5 + 4096, 1024)]
    outB = [rR5h[j].new(f"out{j}", 1)[0] for j in range(2)]
    srow = [srows.ap()[0:1, i * 1024:(i + 1) * 1024] for i in range(4)]
    srowB = [Buf(f"srow{i}") for i in range(4)]

    r1bf = A1.ap()
    x1bf = A2.ap()
    r2fv = [A3.ap().bitcast(F32), A1.ap().bitcast(F32)]

    def stats_tiles(src_ap_fn, src_buf, k, slot):
        tiles, treads = [], []
        for t in range(2):
            tiles.append((ones_sb.ap()[:],
                          src_ap_fn(k, t),
                          ps_ap(slot, t * 512, (t + 1) * 512)[0:1, :],
                          k == 0, k == CH - 1))
            treads.append([src_buf, onesB])
        bd.mm_group(tiles, treads, group_reads=(), writes=[PS[slot]])

    def ln_tail_ops(dst_write, tag, direct_psum=False):
        ops = []
        s1ps = ps_ap(2, 0, 1024)[0:1, :]
        s2ps = ps_ap(3, 0, 1024)[0:1, :]
        ops.append(lambda: bd.op("scalar",
                                 lambda e: e.activation(srow[2], s1ps, AF.Square,
                                                        scale=1.0 / float(np.sqrt(D))),
                                 reads=[PS[2]], writes=[srowB[2]], inc="sc"))
        ops.append(lambda: bd.op("scalar",
                                 lambda e: e.activation(srow[0], s1ps, AF.Identity,
                                                        scale=-1.0 / D),
                                 reads=[PS[2]], writes=[srowB[0]], inc="sc"))
        ops.append(lambda: bd.op("vector", lambda e: e.tensor_sub(srow[2], s2ps, srow[2]),
                                 reads=[PS[3], srowB[2]], writes=[srowB[2]], inc="dv"))
        ops.append(lambda: bd.op("scalar",
                                 lambda e: e.activation(srow[1], srow[2], AF.Ln,
                                                        bias=eps_ap, scale=1.0 / D),
                                 reads=[srowB[2], epsB], writes=[srowB[1]], inc="sc"))
        ops.append(lambda: bd.op("scalar",
                                 lambda e: e.activation(srow[3], srow[1], AF.Exp,
                                                        scale=-0.5),
                                 reads=[srowB[1]], writes=[srowB[3]], inc="sc"))
        hi = [srbf.ap()[0:1, 0:1024], srbf.ap()[0:1, 2048:3072]]
        lo = [srbf.ap()[0:1, 1024:2048], srbf.ap()[0:1, 3072:4096]]
        hiB = [Buf("hi0"), Buf("hi1")]
        loB = [Buf("lo0"), Buf("lo1")]

        def hilo(i, r):
            bd.op("scalar",
                  lambda e, i=i, r=r: e.activation(hi[i], srow[r], AF.Identity),
                  reads=[srowB[r]], writes=[hiB[i]], inc="sc")
            bd.op("vector",
                  lambda e, i=i, r=r: e.tensor_sub(lo[i], srow[r], hi[i]),
                  reads=[srowB[r], hiB[i]], writes=[loB[i]], inc="dv")

        if albe_B[0] is None:
            albe_B[0] = rR2.new("albe", 2)
        albe = albe_B[0]

        def bcast(i, slot):
            tiles, treads = [], []
            for part, pb, st in ((hi, hiB, True), (lo, loB, False)):
                for t in range(2):
                    tiles.append((onesb_sb.ap()[:],
                                  part[i][:, t * 512:(t + 1) * 512],
                                  ps_ap(slot, t * 512, (t + 1) * 512),
                                  st, not st))
                    treads.append([pb[i], onesB])
            bd.mm_group(tiles, treads, group_reads=(), writes=[PS[slot]])
            if not direct_psum:
                bd.op("scalar",
                      lambda e, i=i, slot=slot:
                      e.activation(albe_ap[i], ps_ap(slot, 0, 1024), AF.Identity),
                      reads=[PS[slot]], writes=[albe[i]], inc="sc")


        def bcast32(r, slot):
            tiles, treads = [], []
            for t in range(2):
                tiles.append((onesf_sb.ap()[:],
                              srow[r][:, t * 512:(t + 1) * 512],
                              ps_ap(slot, t * 512, (t + 1) * 512),
                              True, True))
                treads.append([srowB[r], onesB])
            bd.mm_group(tiles, treads, group_reads=(), writes=[PS[slot]])

        mulb = lambda: bd.op("vector",
                             lambda e: e.tensor_mul(srow[2], srow[0], srow[3]),
                             reads=[srowB[0], srowB[3]], writes=[srowB[2]], inc="dv")
        if direct_psum:
            ops.append(lambda: bcast32(3, 2))
            ops.append(mulb)
            ops.append(lambda: bcast32(2, 3))
        else:
            ops.append(lambda: hilo(0, 3))
            ops.append(lambda: bcast(0, 2))
            ops.append(mulb)
            ops.append(lambda: hilo(1, 2))
            ops.append(lambda: bcast(1, 3))
        ops += dst_write()
        return ops

    out_dma_count = [0]

    def bc_ap(slot, rep, width=1024):
        return bass.AP(ps_t[slot], ps_off[slot], [[2048, 128], [0, rep], [1, width]])

    albe_ap = [f32v(A5, R2, 1024), f32v(A5, R2 + 4096, 1024)]
    albe_B = [None]

    def albe_bc(i, rep):
        return albe_ap[i].unsqueeze(1).to_broadcast((128, rep, 1024))

    def make_ln1_apply(x1B, r1B, h):
        ops = []
        for pk in range(4):
            lo, hi_ = h * 8192 + pk * 2048, h * 8192 + (pk + 1) * 2048

            albe = albe_B[0]

            def op1(lo=lo, hi_=hi_, pk=pk, albe=albe):
                r1v3 = A1.ap()[:, lo:hi_].rearrange("p (c t) -> p c t", c=2)
                x1v3 = A2.ap()[:, lo:hi_].rearrange("p (c t) -> p c t", c=2)
                bd.op("vector",
                      lambda e: e.tensor_tensor(x1v3, r1v3, albe_bc(0, 2), ALU.mult),
                      reads=[r1B[2 * pk], r1B[2 * pk + 1], albe[0]],
                      writes=[x1B[2 * pk], x1B[2 * pk + 1]], inc="dv")

            def op2(lo=lo, hi_=hi_, pk=pk, albe=albe):
                x1v3 = A2.ap()[:, lo:hi_].rearrange("p (c t) -> p c t", c=2)
                bd.op("vector",
                      lambda e: e.tensor_tensor(x1v3, x1v3, albe_bc(1, 2), ALU.add),
                      reads=[albe[1], x1B[2 * pk], x1B[2 * pk + 1]],
                      writes=[x1B[2 * pk], x1B[2 * pk + 1]], inc="dv")
            ops += [op1, op2]
        return ops

    ln2u_ap = f32v(A5, R6, 2048)
    ln2u3 = ln2u_ap.rearrange("p (c t) -> p c t", c=2)
    ln2uB_holder = [None]

    def make_ln2_apply(r2B, h, direct_psum=False):
        r2f = r2fv[h]
        if ln2uB_holder[0] is None:
            ln2uB_holder[0] = Buf("ln2u", deps=[b for b in wtB if b is not None])
        u2B = ln2uB_holder[0]
        out3 = f32v(A5, R5, 2048).rearrange("p (c t) -> p c t", c=2)
        al_ap = bc_ap(2, 2) if direct_psum else albe_bc(0, 2)
        be_ap = bc_ap(3, 2) if direct_psum else albe_bc(1, 2)
        if direct_psum:
            a3f = A3.ap().bitcast(F32)
            o3B = [Buf(f"o3_{pk}", r2B_h0_holder[0]) for pk in range(4)]
        ops = []
        for pk in range(4):
            albe = albe_B[0]

            def op_mult(pk=pk, albe=albe):
                r2v3 = r2f[:, pk * 2048:(pk + 1) * 2048].rearrange("p (c t) -> p c t", c=2)
                ar = [PS[2]] if direct_psum else [albe[0]]
                dst = (a3f[:, pk * 2048:(pk + 1) * 2048].rearrange("p (c t) -> p c t", c=2)
                       if direct_psum else ln2u3)
                wr = [o3B[pk]] if direct_psum else [u2B]
                bd.op("vector",
                      lambda e, r2v3=r2v3, dst=dst: e.tensor_tensor(dst, r2v3, al_ap,
                                                                    ALU.mult),
                      reads=[r2B[2 * pk], r2B[2 * pk + 1]] + ar,
                      writes=wr, inc="dv")

            def op_add_store(pk=pk, albe=albe):
                br = [PS[3]] if direct_psum else [albe[1]]
                dst = (a3f[:, pk * 2048:(pk + 1) * 2048].rearrange("p (c t) -> p c t", c=2)
                       if direct_psum else out3)
                rd = [o3B[pk]] if direct_psum else [u2B]
                wr = [o3B[pk]] if direct_psum else [outB[0], outB[1]]
                src0 = dst if direct_psum else ln2u3
                bd.op("vector",
                      lambda e, dst=dst, src0=src0:
                      e.tensor_tensor(dst, src0, be_ap, ALU.add),
                      reads=rd + br, writes=wr, inc="dv")
                for j in range(2):
                    k = 2 * pk + j
                    if h == 1:
                        deng, q = (("scalar", "dO") if k % 2 == 0 else ("sync", "dX"))
                        srcap = a3f[:, k * 1024:(k + 1) * 1024]
                        rb = [o3B[pk]]
                    else:
                        deng, q = (("sync", "dX") if k % 2 == 0 else ("gpsimd", "dG"))
                        srcap = out_ap[j]
                        rb = [outB[j]]
                    bd.op(deng,
                          lambda e, k=k, srcap=srcap:
                          e.dma_start(out_hbm.ap()[k * 128:(k + 1) * 128,
                                                   h * HF:(h + 1) * HF],
                                      srcap),
                          reads=rb, inc=q, step=16)
                    out_dma_count[0] += 1
            ops += [op_mult, op_add_store]
        return ops

    bg = []

    def drain_bg(n):
        for _ in range(n):
            if bg:
                bg.pop(0)()

    def flush_bg():
        while bg:
            bg.pop(0)()


    rsqS6h = [[Buf(f"rsq6_0_{k}", hb) for k in range(CH)]]
    rsqS6h.append([Buf(f"rsq6_1_{k}", rsqS6h[0]) for k in range(CH)])
    rsq6_ap = [x1bf[:, 8192 + k * 1024: 8192 + (k + 1) * 1024] for k in range(CH)]
    sq6_pend = []
    sq6_emitted = [0]

    def emit_sq6_until(j):
        while sq6_emitted[0] <= j and sq6_pend:
            sq6_pend.pop(0)()
            sq6_emitted[0] += 1

    r1B_h, r1view_h = {}, {}
    for h in range(2):
        sq6_pend.clear()
        sq6_emitted[0] = 0
        rsqS6 = rsqS6h[h]
        r1B = rA1.new(f"r1h{h}", CH)
        r1B_h[h] = r1B
        r1view = r1bf[:, h * 8192: (h + 1) * 8192]
        r1view_h[h] = r1view
        xc6 = [new_over(rR1e[2 * j:2 * j + 2], f"xc6h{h}_{j}") for j in range(4)]
        for pre in range(3):
            xc_dma(xc6, pre, h, pre % 4)
        pending_stats = []
        for o in range(CH):
            if o + 3 < CH:
                xc_dma(xc6, o + 3, h, (o + 3) % 4)
            sl = o % 2
            slot = o % 2
            tiles, treads = [], []
            for k in range(CH):
                for t in range(2):
                    tiles.append((wtile(A4.ap()[:, 0:8192], o, k),
                                  A3.ap()[:, k * 2048 + h * HF + t * 512:
                                          k * 2048 + h * HF + (t + 1) * 512],
                                  ps_ap(slot, t * 512, (t + 1) * 512),
                                  k == 0, k == CH - 1))
                    treads.append([g2[k * 2 + h], WldS[o]])
            bd.mm_group(tiles, treads, group_reads=(), writes=[PS[slot]])
            if h == 0 and len(pending_stats) > 1:
                pending_stats.pop(0)()
            if h == 0 and sgn_q:
                sgn_prep(sgn_q.pop(0))
            drain_bg(4)
            xsl = o % 4
            bd.op("scalar",
                  lambda e, sl=sl, xsl=xsl: e.activation(u_ap[sl], xc56_ap[xsl],
                                                         AF.Copy, scale=2.0),
                  reads=[xc6[xsl]], writes=[uB[sl]], inc="sc")
            bd.op("vector",
                  lambda e, slot=slot, sl=sl, o=o, r1view=r1view:
                  e.scalar_tensor_tensor(r1view[:, o * 1024:(o + 1) * 1024],
                                         ps_ap(slot, 0, 1024), vcol("ldb", o),
                                         u_ap[sl], ALU.add, ALU.add),
                  reads=[PS[slot], uB[sl], vecsB], writes=[r1B[o]], inc="dv")
            if h == 0:
                sq6_pend.append(lambda o=o, r1view=r1view, r1B=r1B: bd.op(
                    "scalar",
                    lambda e, o=o, r1view=r1view:
                    e.activation(rsq6_ap[o], r1view[:, o * 1024:(o + 1) * 1024],
                                 AF.Square),
                    reads=[r1B[o]], writes=[rsqS6[o]], inc="sc"))
                if o >= 2:
                    emit_sq6_until(o - 2)

                def _stats(o=o, r1view=r1view, r1B=r1B, rsqS6=rsqS6):
                    emit_sq6_until(o)
                    stats_tiles(lambda k, t, r1view=r1view:
                                r1view[:, k * 1024 + t * 512: k * 1024 + (t + 1) * 512],
                                r1B[o], o, 2)
                    stats_tiles(lambda k, t: rsq6_ap[k][:, t * 512:(t + 1) * 512],
                                rsqS6[o], o, 3)
                pending_stats.append(_stats)
            else:
                sq6_pend.append(lambda o=o, r1view=r1view, r1B=r1B: bd.op(
                    "scalar",
                    lambda e, o=o, r1view=r1view:
                    e.activation(rsq6_ap[o], r1view[:, o * 1024:(o + 1) * 1024],
                                 AF.Square),
                    reads=[r1B[o]], writes=[rsqS6[o]], inc="sc"))
                if o >= 2:
                    emit_sq6_until(o - 2)

                def _stats(o=o, r1view=r1view, r1B=r1B):
                    emit_sq6_until(o)
                    stats_tiles(lambda k, t, r1view=r1view:
                                r1view[:, k * 1024 + t * 512: k * 1024 + (t + 1) * 512],
                                r1B[o], o, 2)
                    stats_tiles(lambda k, t: rsq6_ap[k][:, t * 512:(t + 1) * 512],
                                rsqS6[o], o, 3)
                pending_stats.append(_stats)
                if o >= 4 and len(bg) <= 8:
                    while len(pending_stats) > (CH - 1 - o):
                        pending_stats.pop(0)()
        for f in pending_stats:
            f()
        if h == 0:
            x1B0 = rA2.new("x1h0", CH)

            def ln1a_write(x1B=x1B0, r1B=r1B, h=0):
                return make_ln1_apply(x1B, r1B, h)

            bg += ln_tail_ops(ln1a_write, "ln1h0")

    flush_bg()

    r1B = r1B_h[1]
    r1view = r1view_h[1]
    x1B1 = rA2.new("x1h1", CH)

    def ln1b_write(x1B=x1B1, r1B=r1B, h=1):
        return make_ln1_apply(x1B, r1B, h)

    bg += ln_tail_ops(ln1b_write, "ln1h1")

    x1B_h = {0: x1B0, 1: x1B1}
    r2B_h0_holder = [None]
    for h in range(2):
        x1B = x1B_h[h]
        x1view = x1bf[:, h * 8192:(h + 1) * 8192]
        r2f = r2fv[h]
        r2B = rA3.new("r2h0", CH) if h == 0 else rA1.new("r2h1", CH)
        if h == 0:
            r2B_h0_holder[0] = r2B
        rsqS = rA4lo.new(f"rsq_h{h}", CH)
        rsq_ap8 = [A4.ap()[:, k * 1024:(k + 1) * 1024] for k in range(CH)]
        sq_ops = []
        deferred2 = []
        r2bfB = [new_over(rR1e[k:k + 1], f"r2bfh{h}_{k}") for k in range(CH)]
        r2bf_ap8 = [A5.ap()[:, R1 // 2 + k * 1024: R1 // 2 + (k + 1) * 1024]
                    for k in range(CH)]
        deferred = []
        for o in range(CH):
            sl = o % 2
            slot = o % 2
            tiles, treads = [], []
            for k in range(CH):
                for t in range(2):
                    tiles.append((wtile(A4.ap()[:, 8192:16384], o, k),
                                  x1view[:, k * 1024 + t * 512: k * 1024 + (t + 1) * 512],
                                  ps_ap(slot, t * 512, (t + 1) * 512),
                                  k == 0, k == CH - 1))
                    treads.append([x1B[k], sgnB])
            bd.mm_group(tiles, treads, group_reads=(), writes=[PS[slot]])
            while len(bg) > max(0, 6 - 2 * (o // 2)):
                bg.pop(0)()
            if not bg and deferred:
                deferred.pop(0)()
            bd.op("scalar",
                  lambda e, slot=slot, sl=sl, o=o:
                  e.activation(u_ap[sl], ps_ap(slot, 0, 1024), AF.Relu,
                               bias=dvec_sb[:, 32 + o:33 + o],
                               scale=dvec_sb[:, 24 + o:25 + o]),
                  reads=[PS[slot], dvecB], writes=[uB[sl]], inc="sc")
            bd.op("scalar",
                  lambda e, sl=sl:
                  e.activation(u_ap[sl], u_ap[sl], AF.Relu,
                               bias=dvec_sb[:, 40:41], scale=-1.0),
                  reads=[uB[sl], dvecB], writes=[uB[sl]], inc="sc")
            bd.op("vector",
                  lambda e, sl=sl, o=o, x1view=x1view, r2f=r2f:
                  e.scalar_tensor_tensor(r2f[:, o * 1024:(o + 1) * 1024],
                                         u_ap[sl], -1.0,
                                         x1view[:, o * 1024:(o + 1) * 1024],
                                         ALU.mult, ALU.add),
                  reads=[uB[sl], x1B[o]], writes=[r2B[o]], inc="dv")
            bd.op("vector",
                  lambda e, o=o, r2f=r2f: e.tensor_copy(r2bf_ap8[o],
                                               r2f[:, o * 1024:(o + 1) * 1024]),
                  reads=[r2B[o]], writes=[r2bfB[o]], inc="dv")
            sq_ops.append(lambda o=o: bd.op(
                "scalar",
                lambda e, o=o: e.activation(rsq_ap8[o], r2bf_ap8[o], AF.Square),
                reads=[r2bfB[o]], writes=[rsqS[o]], inc="sc"))
            if o >= 2:
                sq_ops.pop(0)()

            def _sum(o=o):
                stats_tiles(lambda k, t: r2bf_ap8[k][:, t * 512:(t + 1) * 512],
                            r2bfB[o], o, 2)

            def _sq(o=o):
                stats_tiles(lambda k, t: rsq_ap8[k][:, t * 512:(t + 1) * 512],
                            rsqS[o], o, 3)
            deferred.append(_sum)
            deferred2.append(_sq)
        flush_bg()
        while sq_ops:
            sq_ops.pop(0)()
        for f in deferred:
            f()
        for f in deferred2:
            f()

        def ln2_write(r2B=r2B, h=h):
            return make_ln2_apply(r2B, h, direct_psum=(h == 1))

        bg += ln_tail_ops(ln2_write, f"ln2h{h}", direct_psum=(h == 1))

    flush_bg()

    def final_wait(e):
        for base in ("dO", "dX", "dG"):
            for j in range(Builder.DMA_POOL):
                nm = f"{base}{j}"
                if bd.counts.get(nm, 0):
                    e.wait_ge(bd.sems[nm], bd.counts[nm])
    bd.raw("scalar", final_wait)

    with nc.Block() as block:
        @block.tensor
        def _(e):
            for th in bd.q["tensor"]:
                th(e)

        @block.vector
        def _(e):
            for th in bd.q["vector"]:
                th(e)

        @block.scalar
        def _(e):
            for th in bd.q["scalar"]:
                th(e)

        @block.sync
        def _(e):
            for th in bd.q["sync"]:
                th(e)

        @block.gpsimd
        def _(e):
            for th in bd.q["gpsimd"]:
                th(e)

    nc.finalize()
    return nc


_NC_CACHE = {}


def _blocked(w):
    return np.ascontiguousarray(
        w.reshape(8, 128, 8, 128).transpose(1, 2, 0, 3).reshape(128, 8192))


def kernel(L0_attn_x, L0_attn_fwd_a_logit, L0_attn_fwd_Wx, L0_attn_fwd_bx,
           L0_attn_fwd_Wgx, L0_attn_fwd_bgx, L0_attn_fwd_Wga, L0_attn_fwd_bga,
           L0_attn_fwd_Wo, L0_attn_fwd_bo, L0_attn_bwd_a_logit, L0_attn_bwd_Wx,
           L0_attn_bwd_bx, L0_attn_bwd_Wgx, L0_attn_bwd_bgx, L0_attn_bwd_Wga,
           L0_attn_bwd_bga, L0_attn_bwd_Wo, L0_attn_bwd_bo, L0_attn_ld_W,
           L0_attn_ld_b, L0_ffn_weights, L0_ffn_scale, L0_ffn_bias):
    x = np.asarray(L0_attn_x, dtype=np.float32)
    assert x.shape == (B, S, D)

    if "nc" not in _NC_CACHE:
        _NC_CACHE["nc"] = build_program()
    nc = _NC_CACHE["nc"]

    wmap = {
        "Wxf": _blocked(np.asarray(L0_attn_fwd_Wx, np.float32)),
        "Wxb": _blocked(np.asarray(L0_attn_bwd_Wx, np.float32)),
        "Wgxf": _blocked(np.asarray(L0_attn_fwd_Wgx, np.float32)),
        "Wgxb": _blocked(np.asarray(L0_attn_bwd_Wgx, np.float32)),
        "Wgaf": _blocked(np.asarray(L0_attn_fwd_Wga, np.float32)),
        "Wgab": _blocked(np.asarray(L0_attn_bwd_Wga, np.float32)),
        "Wof": _blocked(np.asarray(L0_attn_fwd_Wo, np.float32)),
        "Wob": _blocked(np.asarray(L0_attn_bwd_Wo, np.float32)),
        "Wld": _blocked(np.asarray(L0_attn_ld_W, np.float32)),
        "Wffn": _blocked(np.ascontiguousarray(np.asarray(L0_ffn_weights, np.float32).T)),
    }
    vecs = np.zeros((128, 8 * len(VEC_NAMES)), np.float32)
    vv = {
        "af": L0_attn_fwd_a_logit, "ab": L0_attn_bwd_a_logit,
        "bxf": L0_attn_fwd_bx, "bxb": L0_attn_bwd_bx,
        "bgxf": L0_attn_fwd_bgx, "bgxb": L0_attn_bwd_bgx,
        "bgaf": L0_attn_fwd_bga, "bgab": L0_attn_bwd_bga,
        "bof": L0_attn_fwd_bo, "bob": L0_attn_bwd_bo,
        "ldb": L0_attn_ld_b, "fsc": L0_ffn_scale, "fbi": L0_ffn_bias,
    }
    for nm, v in vv.items():
        vecs[:, VCOL[nm]:VCOL[nm] + 8] = np.asarray(v, np.float32).reshape(8, 128).T

    in_maps = []
    for b in range(B):
        m = {"x": np.ascontiguousarray(x[b].T), "vecs": vecs}
        m.update(wmap)
        in_maps.append(m)

    res = run_bass_kernel_spmd(nc, in_maps, core_ids=list(range(B)))
    out = np.empty((B, S, D), np.float32)
    for b in range(B):
        out[b] = np.asarray(res.results[b]["out"]).T
    return out
